# version 09
# baseline (speedup 1.0000x reference)
from contextlib import ExitStack

import numpy as np
import ml_dtypes
import concourse.bass as bass
import concourse.mybir as mybir
from concourse.bass_utils import run_bass_kernel_spmd

F32 = mybir.dt.float32
BF16 = mybir.dt.bfloat16
I32 = mybir.dt.int32
AF = mybir.ActivationFunctionType
ALU = mybir.AluOpType
AX = mybir.AxisListType

P = 128
D = 1024
S = 2048
NT = S // P
NQ = 4
TPQ = 4
NB = 2
NH = 16
HD = 64
NC = 8
VW = HD + 1


def _f(x):
    return float(np.float32(x))


def _fa(x):
    return np.float32(x)


class Sem:
    _next_id = 0

    def __init__(self, handle):
        self.h = handle
        self.n = 0
        self.id = Sem._next_id
        Sem._next_id += 1

    def inc(self, instr, k=1):
        instr.then_inc(self.h, k)
        self.n += k
        return self.n


class WaitCache:

    def __init__(self, c):
        self.c = c
        self.seen = {}

    def wait(self, eng, eng_name, sem, val):
        if val <= 0:
            return
        key = (eng_name, sem.id)
        if self.seen.get(key, -1) >= val:
            return
        eng.wait_ge(sem.h, val)
        self.seen[key] = val


def twait(c, sem, val):
    c.wcache.wait(c.nc.tensor, "t", sem, val)


def vwait(c, sem, val):
    c.wcache.wait(c.nc.vector, "v", sem, val)


def swait(c, sem, val):
    c.wcache.wait(c.nc.scalar, "s", sem, val)


def gwait(c, sem, val):
    c.wcache.wait(c.nc.gpsimd, "g", sem, val)


def ywait(c, sem, val):
    c.wcache.wait(c.nc.sync, "y", sem, val)


class Ctx:
    pass


def _gap(c, n=3):
    ins = None
    for _ in range(n):
        ins = c.nc.vector.tensor_copy(c.rows[:, 6:7], c.rows[:, 7:8])
    return ins


def _host_constants():
    t = np.arange(S, dtype=np.float64)
    inv = 1.0 / (10000.0 ** (np.arange(0, D, 2, dtype=np.float64) / D))
    ang = np.outer(t, inv)
    cst = np.empty((D, S), dtype=np.float32)
    cst[: D // 2] = np.cos(ang).T
    cst[D // 2:] = np.sin(ang).T
    ident = np.eye(P, dtype=ml_dtypes.bfloat16)
    ones_bf = np.ones((P, P), dtype=ml_dtypes.bfloat16)
    ones_f32 = np.ones((1, P), dtype=np.float32)
    return cst, ident, ones_bf, ones_f32


def build_program(upto=4, debug=None):
    nc = bass.Bass("TRN2", target_bir_lowering=False)
    c = Ctx()
    c.nc = nc
    c.upto = upto
    c.debug = debug

    c.x = nc.dram_tensor("x", [NB * S, D], F32, kind="ExternalInput").ap()
    c.wqa = nc.dram_tensor("wqa", [D, D], F32, kind="ExternalInput").ap()
    c.wka = nc.dram_tensor("wka", [D, D], F32, kind="ExternalInput").ap()
    c.wva = nc.dram_tensor("wva", [D, D], F32, kind="ExternalInput").ap()
    c.woa = nc.dram_tensor("woa", [D, D], F32, kind="ExternalInput").ap()
    c.wqtp = nc.dram_tensor("wqtp", [D, D], F32, kind="ExternalInput").ap()
    c.wktp = nc.dram_tensor("wktp", [D, D], F32, kind="ExternalInput").ap()
    c.wvt = nc.dram_tensor("wvt", [D, D], F32, kind="ExternalInput").ap()
    c.wot = nc.dram_tensor("wot", [D, D], F32, kind="ExternalInput").ap()
    c.wd = nc.dram_tensor("wd", [D, D], F32, kind="ExternalInput").ap()
    c.bias = nc.dram_tensor("bias", [1, D], F32, kind="ExternalInput").ap()
    c.cst = nc.dram_tensor("cst", [D, S], F32, kind="ExternalInput").ap()
    c.identc = nc.dram_tensor("identc", [P, P], BF16, kind="ExternalInput").ap()
    c.onesb = nc.dram_tensor("onesb", [P, HD], BF16, kind="ExternalInput").ap()
    c.onesf = nc.dram_tensor("onesf", [1, P], F32, kind="ExternalInput").ap()
    c.out = nc.dram_tensor("out", [NB * S, D], F32, kind="ExternalOutput").ap()
    if debug:
        c.dbg1 = nc.dram_tensor("dbg1", [P, 8 * S], BF16, kind="ExternalOutput").ap()
        c.dbg2 = nc.dram_tensor("dbg2", [P, 8 * S], BF16, kind="ExternalOutput").ap()
        c.dbg3 = nc.dram_tensor("dbg3", [P, NT * NH * VW], BF16, kind="ExternalOutput").ap()
        c.dbg7 = nc.dram_tensor("dbg7", [P, 8 * S], BF16, kind="ExternalOutput").ap()
        c.dbg8 = nc.dram_tensor("dbg8", [P, 8 * S], BF16, kind="ExternalOutput").ap()
        c.dbg9 = nc.dram_tensor("dbg9", [P, NT * NH * VW], BF16, kind="ExternalOutput").ap()
        c.dbg4 = nc.dram_tensor("dbg4", [P, 4 * NT], F32, kind="ExternalOutput").ap()
        c.dbg5 = nc.dram_tensor("dbg5", [P, 4096], BF16, kind="ExternalOutput").ap()
        c.dbg6 = nc.dram_tensor("dbg6", [P, 96], F32, kind="ExternalOutput").ap()
    c.wdt_scr = nc.dram_tensor("wdt_scr", [D, D], BF16, kind="Internal").ap()

    with ExitStack() as st:
        E = st.enter_context
        c.resid = E(nc.sbuf_tensor("resid", [P, NT * D], F32))
        c.qt = E(nc.sbuf_tensor("qt", [P, 8 * S], BF16))
        c.kt = E(nc.sbuf_tensor("kt", [P, 8 * S], BF16))
        c.vv = E(nc.sbuf_tensor("vv", [P, NT * NH * VW], BF16))
        c.xnt = E(nc.sbuf_tensor("xnt", [P, 8 * 512], BF16))
        c.wdq = E(nc.sbuf_tensor("wdq", [P, 8 * D], BF16))
        c.wbuf = E(nc.sbuf_tensor("wbuf", [P, 3 * D], BF16))
        c.scr = E(nc.sbuf_tensor("scr", [P, 2 * D], F32))
        c.esb = E(nc.sbuf_tensor("esb", [P, 4 * D], BF16))
        c.ident = E(nc.sbuf_tensor("ident", [P, P], BF16))
        c.ones = E(nc.sbuf_tensor("ones", [P, HD], BF16))
        c.stats = E(nc.sbuf_tensor("stats", [P, 4 * NT], F32))
        c.rows = E(nc.sbuf_tensor("rows", [P, 8], F32))
        c.qs = E(nc.sbuf_tensor("qs", [P, 160], F32))
        vvap = c.vv[:, :]
        c.xns = vvap[:, 0:4096].bitcast(F32)
        c.qv = vvap[:, 4096:6144].bitcast(F32)
        c.qm = vvap[:, 6144:8192]
        c.bias_bc = vvap[:, 8192:10240].bitcast(F32)
        c.qzb = vvap[:, 10240:11264]
        c.csb = c.esb[:, 0:2048]
        c.xnb = c.esb[:, 2048:4096]
        c.sqscr = c.esb[:, 0:2048].bitcast(F32)
        c.bcsb = c.xnt[:, 0:1024]
        c.rrow = c.xnt[0:1, 1024:2048]
        c.nsc = c.scr[:, 1024:2048].bitcast(BF16)
        c.rrow32 = c.xnt[0:1, 2048:4096].bitcast(F32)
        c.onesf_sb = c.scr[0:1, 1536:1664]
        c.bias_row = c.scr[0:1, 256:1280]

        c.s_pe = Sem(E(nc.semaphore("s_pe")))
        c.s_act = Sem(E(nc.semaphore("s_act")))
        c.s_dve = Sem(E(nc.semaphore("s_dve")))
        c.d_w = Sem(E(nc.semaphore("d_w")))
        c.d_x = Sem(E(nc.semaphore("d_x")))
        c.d_st = Sem(E(nc.semaphore("d_st")))
        c.d_wsl = [Sem(E(nc.semaphore("d_ws%d" % i))) for i in range(3)]
        c.d_csl = [Sem(E(nc.semaphore("d_cs%d" % i))) for i in range(2)]

        c.wcache = WaitCache(c)
        c.wslot = 0
        c.wfree = {}
        c.store_done = 0
        c.norm_ctr = 0
        _emit(c)
    return nc


def _emit(c):
    nc = c.nc
    for ap_sb, ap_hbm in ((c.ident[:, :], c.identc), (c.ones[:, :], c.onesb)):
        nc.sync.dma_start(ap_sb, ap_hbm).then_inc(c.d_x.h, 16)
        c.d_x.n += 16
    c.const_ld = c.d_x.n
    ini = c.s_dve.inc(nc.vector.memset(c.rows[:, :], 0.0))
    swait(c, c.s_dve, ini)
    twait(c, c.s_dve, ini)
    gwait(c, c.s_dve, ini)
    ywait(c, c.s_dve, ini)

    _ffn_weight_prep(c)

    for b in range(NB):
        _load_x(c, b)
        if c.upto == -2:
            _ln_stats(c, b)
            c._skip_finish = True
            _ffn(c, b, 0)
            c._skip_finish = False
            _store_x(c, b)
            continue
        _attn_a(c, b)
        if c.upto >= 2:
            _ffn(c, b, 0, fuse_next=(c.upto >= 3))
        if c.upto >= 3:
            _attn_b(c, b)
        if c.upto >= 4:
            _ffn(c, b, 1)
        _store_x(c, b)
    ywait(c, c.d_st, c.d_st.n)


def _load_x(c, b):
    nc = c.nc
    if b > 0:
        ywait(c, c.d_st, c.store_done)
    src = c.x[b * S:(b + 1) * S, :].rearrange("(t p) d -> p t d", p=P)
    dst = c.resid[:, :].rearrange("p (t d) -> p t d", d=D)
    nc.sync.dma_start(dst, src).then_inc(c.d_x.h, 16)
    c.d_x.n += 16
    c.x_ready = c.d_x.n


def _store_x(c, b):
    nc = c.nc
    dst = c.out[b * S:(b + 1) * S, :].rearrange("(t p) d -> p t d", p=P)
    src = c.resid[:, :].rearrange("p (t d) -> p t d", d=D)
    ywait(c, c.s_dve, c.s_dve.n)
    nc.sync.dma_start(dst, src).then_inc(c.d_st.h, 16)
    c.d_st.n += 16
    c.store_done = c.d_st.n


def _wload(c, src_ap, ncols=D):
    nc = c.nc
    slot = c.wslot % 3
    c.wslot += 1
    free_at = c.wfree.get(slot)
    if free_at is not None:
        gwait(c, c.s_pe, free_at)
    dst = c.wbuf[:, slot * D: slot * D + ncols]
    sem = c.d_wsl[slot]
    nc.gpsimd.dma_start(dst, src_ap).then_inc(sem.h, 16)
    sem.n += 16
    return dst, (sem, sem.n), slot


def _wdq_load(c, wsrc, pe_free, dma_wait=None):
    nc = c.nc
    gwait(c, c.s_pe, pe_free)
    if dma_wait is not None:
        gwait(c, c.d_x, dma_wait)
    src = wsrc.rearrange("(j p) e -> p j e", p=P)
    dst = c.wdq[:, :].rearrange("p (j e) -> p j e", e=D)
    nc.gpsimd.dma_start(dst, src).then_inc(c.d_w.h, 16)
    c.d_w.n += 16
    return c.d_w.n


def _ffn_weight_prep(c):
    nc = c.nc
    prev_evac = 0
    for t in range(8):
        ywait(c, c.d_x, c.const_ld)
        if prev_evac:
            ywait(c, c.s_dve, c.s_dve.n)
        nc.sync.dma_start(c.scr[:, 0:D], c.wd[t * P:(t + 1) * P, :]).then_inc(c.d_x.h, 16)
        c.d_x.n += 16
        vwait(c, c.d_x, c.d_x.n)
        _quant_tile(c, c.scr[:, 0:D], c.qzb[:, :], m2_buf=c.bias_bc[:, 0:D])
        dv = c.s_dve.n
        with nc.psum_tensor("fwA%d" % t, [P, 4 * P], BF16) as pa, \
                nc.psum_tensor("fwB%d" % t, [P, 4 * P], BF16) as pbx:
            for g in range(2):
                ps = (pa, pbx)[g]
                twait(c, c.s_dve, dv)
                if prev_evac:
                    twait(c, c.s_dve, prev_evac)
                mm = None
                for j4 in range(4):
                    j = g * 4 + j4
                    mm = nc.tensor.transpose(ps[:, j4 * P:(j4 + 1) * P],
                                             c.qzb[:, j * P:(j + 1) * P],
                                             c.ident[:, :])
                pe = c.s_pe.inc(mm)
                vwait(c, c.s_pe, pe)
                dst = c.wdq[:, :].rearrange("p (j d) -> p j d", d=D)[
                    :, g * 4:(g + 1) * 4, t * P:(t + 1) * P]
                src = ps[:, :].rearrange("p (j q) -> p j q", q=P)
                prev_evac = c.s_dve.inc(nc.vector.tensor_copy(dst, src))
    ywait(c, c.s_dve, c.s_dve.n)
    src = c.wdq[:, :].rearrange("p (j d) -> p j d", d=D)
    dst = c.wdt_scr.rearrange("(j p) d -> p j d", p=P)
    nc.sync.dma_start(dst, src).then_inc(c.d_x.h, 16)
    c.d_x.n += 16
    c.wdt_written = c.d_x.n


def _quant_tile(c, xin, xout, mid_fn=None, m2_buf=None):
    nc = c.nc
    v3 = xin.rearrange("p (b k) -> p b k", k=32)
    am = c.qs[:, 0:32]
    sc = c.qs[:, 32:64]
    rs = c.qs[:, 64:96]
    nc.vector.tensor_reduce(am, v3, AX.X, ALU.max, apply_absolute_value=True)
    _gap(c)
    a7 = float(np.float32(1.0 / 7.5))
    b7 = float(np.float32(np.float64(1.0) / 7.5 - np.float64(np.float32(1.0 / 7.5))))
    sc2 = c.qs[:, 3 * 32:4 * 32]
    nc.vector.tensor_scalar(sc2, am, _f(a7), None, ALU.mult)
    _gap(c)
    nc.vector.scalar_tensor_tensor(sc2, am, _f(b7), sc2, ALU.mult, ALU.add)
    _gap(c)
    nc.vector.tensor_scalar(sc, sc2, _f(1e-30), None, ALU.max)
    _gap(c)
    nc.vector.reciprocal(rs, sc)
    _gap(c)
    rs_b = rs.unsqueeze(2).broadcast_to([P, 32, 32])
    sc_b = sc.unsqueeze(2).broadcast_to([P, 32, 32])
    qv3 = c.qv[:, :].rearrange("p (b k) -> p b k", k=32)
    nc.vector.tensor_tensor(qv3, v3, rs_b, ALU.mult)
    m1 = c.qm[:, 0:D]
    m2 = c.qm[:, D:2 * D]
    kk = c.qzb[:, :]
    av = c.nsc[:, 0:1024]
    nc.vector.scalar_tensor_tensor(av, c.qv[:, :], _f(-1.0), c.qv[:, :],
                                   ALU.mult, ALU.max)
    if mid_fn is not None:
        mid_fn()
    nc.vector.tensor_scalar(m1, av, _f(2.0), _f(4.0), ALU.is_ge, ALU.mult)
    nc.vector.tensor_scalar(m2, av, _f(4.0), _f(2.0), ALU.is_ge, ALU.mult)
    nc.vector.tensor_tensor(kk, m1, m2, ALU.add)
    nc.vector.tensor_scalar(kk, kk, _f(-1.0), _f(8.0), ALU.mult, ALU.add)
    yy = c.scr[:, D:2 * D]
    nc.vector.tensor_tensor(yy, c.qv[:, :], kk, ALU.mult)
    zi = c.scr[:, 0:D].bitcast(I32)
    nc.vector.tensor_copy(zi, yy)
    nc.vector.tensor_copy(c.qzb[:, :], zi)
    sg = m1
    nc.vector.tensor_scalar(sg, m1, _f(0.03125), _f(0.125), ALU.mult, ALU.add)
    nc.vector.scalar_tensor_tensor(sg, m2, _f(0.125), sg, ALU.mult, ALU.add)
    nc.vector.tensor_tensor(c.qzb[:, :], c.qzb[:, :], sg, ALU.mult)
    q3 = c.qzb[:, :].rearrange("p (b k) -> p b k", k=32)
    o3 = xout.rearrange("p (b k) -> p b k", k=32)
    nc.vector.tensor_tensor(o3, q3, sc_b, ALU.mult)
    return c.s_dve.inc(_gap(c, 3))


def _ln_accum(c, t, dve_wait=None):
    nc = c.nc
    tile = c.resid[:, t * D:(t + 1) * D]
    i1 = nc.vector.tensor_scalar(c.scr[:, 0:D], tile, _f(1.0), _f(0.0),
                                 ALU.mult, ALU.add,
                                 accum_out=c.stats[:, t:t + 1])
    c.s_dve.inc(i1)
    if dve_wait:
        swait(c, c.s_dve, dve_wait)
    i2 = nc.scalar.activation(c.sqscr[:, 0:D], tile, AF.Square,
                              accum_out=c.stats[:, NT + t:NT + t + 1])
    c.s_act.inc(i2)


def _ln_finish(c):
    nc = c.nc
    if getattr(c, "_skip_finish", False):
        return
    ac_sq = c.s_act.inc(nc.scalar.activation(c.rows[:, 4:5], c.rows[:, 0:1],
                                             AF.Square))
    mean = c.stats[:, 2 * NT:3 * NT]
    rstd = c.stats[:, 3 * NT:4 * NT]
    tmp = c.qs[:, 0:NT]
    vwait(c, c.s_act, ac_sq)
    _gap(c, 2)
    nc.vector.tensor_scalar(mean, c.stats[:, 0:NT], _f(1.0 / D), None, ALU.mult)
    _gap(c)
    nc.vector.tensor_tensor(tmp, mean, mean, ALU.mult)
    _gap(c)
    var = c.qs[:, 2 * NT:3 * NT]
    nc.vector.scalar_tensor_tensor(var, c.stats[:, NT:2 * NT], _f(1.0 / D), tmp,
                                   ALU.mult, ALU.subtract)
    dv_var = c.s_dve.inc(_gap(c, 3))
    swait(c, c.s_dve, dv_var)
    nc.scalar.activation(c.qs[:, NT:2 * NT], var, AF.Sqrt, bias=_fa(1e-5))
    ac_sq2 = c.s_act.inc(nc.scalar.activation(c.rows[:, 4:5], c.rows[:, 0:1],
                                              AF.Square))
    vwait(c, c.s_act, ac_sq2)
    nc.vector.reciprocal(rstd, c.qs[:, NT:2 * NT])
    _gap(c)
    nc.vector.scalar_tensor_tensor(c.stats[:, 0:NT], mean, _f(-1.0), rstd,
                                   ALU.mult, ALU.mult)
    dv_nb = c.s_dve.inc(_gap(c, 3))
    c.ln_ready = dv_nb
    c.ln_act_end = c.s_act.n


def _ln_stats(c, b):
    nc = c.nc
    vwait(c, c.d_x, c.x_ready)
    swait(c, c.d_x, c.x_ready)
    swait(c, c.s_dve, c.s_dve.n)
    for t in range(NT):
        _ln_accum(c, t)
    _ln_finish(c)


def _ln_apply(c, t, out_ap):
    nc = c.nc
    swait(c, c.s_dve, c.ln_ready)
    nc.scalar.activation(out_ap, c.resid[:, t * D:(t + 1) * D], AF.Identity,
                         bias=c.stats[:, t:t + 1],
                         scale=c.stats[:, 3 * NT + t:3 * NT + t + 1])
    ins = nc.scalar.activation(c.rows[:, 4:5], c.rows[:, 0:1], AF.Square)
    return c.s_act.inc(ins)


class TransPair:
    def __init__(self, c, name, stack):
        self.c = c
        self.ps = [stack.enter_context(c.nc.psum_tensor(name + "A", [P, 4 * P], BF16)),
                   stack.enter_context(c.nc.psum_tensor(name + "B", [P, 4 * P], BF16))]
        self.last_evac = [c.s_dve.n, c.s_dve.n]
        self.i = 0

    def transpose8(self, src_bf16, dst_fn, pe_wait=None):
        c, nc = self.c, self.c.nc
        for g in range(2):
            ps = self.ps[self.i % 2]
            if self.last_evac[self.i % 2]:
                twait(c, c.s_dve, self.last_evac[self.i % 2])
            if pe_wait is not None:
                twait(c, *pe_wait)
                pe_wait = None
            mm = None
            for j4 in range(4):
                j = g * 4 + j4
                mm = nc.tensor.transpose(ps[:, j4 * P:(j4 + 1) * P],
                                         src_bf16[:, j * P:(j + 1) * P],
                                         c.ident[:, :])
            pe = c.s_pe.inc(mm)
            vwait(c, c.s_pe, pe)
            ins = None
            for j4 in range(4):
                j = g * 4 + j4
                ins = nc.vector.tensor_copy(dst_fn(j), ps[:, j4 * P:(j4 + 1) * P])
            self.last_evac[self.i % 2] = c.s_dve.inc(ins)
            self.i += 1
        return c.s_dve.n


def _attn_a(c, b):
    nc = c.nc
    _ln_stats(c, b)
    if c.debug == "proj" and b == 0:
        ywait(c, c.s_dve, c.s_dve.n)
        nc.sync.dma_start(c.dbg6, c.qs[:, 0:96]).then_inc(c.d_st.h, 16)
        c.d_st.n += 16
    wva_ld = _wdq_load(c, c.wva, c.s_pe.n, dma_wait=c.wdt_written)

    qt3 = c.qt[:, :].rearrange("p (e s) -> p e s", e=8)
    kt3 = c.kt[:, :].rearrange("p (e s) -> p e s", e=8)
    vv4 = c.vv[:, :].rearrange("p (t h v) -> p t h v", t=NT, h=NH)

    with ExitStack() as stk:
        tp = TransPair(c, "ta%d" % b, stk)
        pq = [stk.enter_context(nc.psum_tensor("qkA%d" % b, [P, 512], F32)),
              stk.enter_context(nc.psum_tensor("qkB%d" % b, [P, 512], F32))]
        pq_evac = [0, 0]
        xnb_free = [0, 0]
        for quar in range(NQ):
            for t_rel in range(TPQ):
                t = quar * TPQ + t_rel
                xn_tile = c.xnb[:, (t % 2) * D:(t % 2) * D + D]
                if xnb_free[t % 2]:
                    swait(c, c.s_pe, xnb_free[t % 2])
                ac = _ln_apply(c, t, xn_tile)
                tp.transpose8(
                    xn_tile,
                    lambda j, t_rel=t_rel: c.xnt[:, j * 512 + t_rel * P:
                                                 j * 512 + t_rel * P + P],
                    pe_wait=(c.s_act, ac))
                xnb_free[t % 2] = c.s_pe.n
            dv_xnt = c.s_dve.inc(_gap(c, 2))
            if c.debug == "proj" and b == 0 and quar == 0:
                ywait(c, c.s_dve, dv_xnt)
                nc.sync.dma_start(c.dbg5, c.xnt[:, :]).then_inc(c.d_st.h, 16)
                c.d_st.n += 16
            for proj, wsrc, dst_sb in ((0, c.wqa, c.qt), (1, c.wka, c.kt)):
                for et in range(8):
                    src = wsrc[:, et * P:(et + 1) * P].rearrange(
                        "(j p) e -> p j e", p=P)
                    slot_ap, wv, slot = _wload(c, src)
                    slot3 = slot_ap.rearrange("p (j e) -> p j e", e=P)
                    buf = et % 2
                    twait(c, wv[0], wv[1])
                    twait(c, c.s_dve, dv_xnt)
                    if pq_evac[buf]:
                        twait(c, c.s_act, pq_evac[buf])
                    mm = None
                    for j in range(8):
                        mm = nc.tensor.matmul(
                            pq[buf][:, :], slot3[:, j, :],
                            c.xnt[:, j * 512:(j + 1) * 512],
                            start=(j == 0), stop=(j == 7))
                    pe = c.s_pe.inc(mm)
                    c.wfree[slot] = pe
                    swait(c, c.s_pe, pe)
                    ins = nc.scalar.activation(
                        dst_sb[:, et * S + quar * 512: et * S + quar * 512 + 512],
                        pq[buf][:, :], AF.Copy)
                    pq_evac[buf] = c.s_act.inc(ins)
            buf = 0
            for t_rel in range(TPQ):
                t = quar * TPQ + t_rel
                lhsT3 = c.xnt[:, :].rearrange("p (j q) -> p j q", q=512)
                for half in range(2):
                    twait(c, c.d_w, wva_ld)
                    if pq_evac[buf]:
                        twait(c, c.s_act, pq_evac[buf])
                    mm = None
                    for j in range(8):
                        mm = nc.tensor.matmul(
                            pq[buf][:, :], lhsT3[:, j, t_rel * P:(t_rel + 1) * P],
                            c.wdq[:, j * D + half * 512: j * D + (half + 1) * 512],
                            start=(j == 0), stop=(j == 7))
                    pe = c.s_pe.inc(mm)
                    swait(c, c.s_pe, pe)
                    dst = vv4[:, t, half * 8:(half + 1) * 8, 0:HD]
                    src3 = pq[buf][:, :].rearrange("p (h v) -> p h v", h=8)
                    pq_evac[buf] = c.s_act.inc(
                        nc.scalar.activation(dst, src3, AF.Copy))
                    buf = (buf + 1) % 2
    vones = vv4[:, :, :, HD:VW]
    nc.vector.memset(vones, 1.0)
    dv_v = c.s_dve.inc(_gap(c, 3))
    for _ in range(2):
        ac_v = c.s_act.inc(nc.scalar.activation(c.rows[:, 4:5], c.rows[:, 0:1],
                                                AF.Square))

    if c.debug == "proj":
        ywait(c, c.s_dve, c.s_dve.n)
        ywait(c, c.s_act, c.s_act.n)
        dsts = ((c.dbg1, c.dbg2, c.dbg3) if b == 0 else (c.dbg7, c.dbg8, c.dbg9))
        for dsrc, ddst in ((c.qt[:, :], dsts[0]), (c.kt[:, :], dsts[1]),
                           (c.vv[:, :], dsts[2])):
            nc.sync.dma_start(ddst, dsrc).then_inc(c.d_st.h, 16)
            c.d_st.n += 16
        vwait(c, c.d_st, c.d_st.n)
        swait(c, c.d_st, c.d_st.n)
        twait(c, c.d_st, c.d_st.n)

    woa_ld = _wdq_load(c, c.woa, c.s_pe.n)

    with nc.psum_tensor("spA%d" % b, [P, 1024], F32) as sp0, \
            nc.psum_tensor("spB%d" % b, [P, 1024], F32) as sp1, \
            nc.psum_tensor("upp%d" % b, [VW, 1024], F32) as up, \
            nc.psum_tensor("bcp%d" % b, [HD, 1024], F32) as bc:
        sp = (sp0, sp1)
        AVLAG = 3
        sbuf_last_exp = [0, 0]
        eslot_last_av = [0] * 4
        norm_done = 0
        chain_done = 0
        pend_norm = None
        twait(c, c.s_dve, dv_v)
        twait(c, c.s_act, ac_v)
        for hp in range(8):
            for qb in range(4):
                exp_of = {}

                def emit_scores(kt, hp=hp, qb=qb):
                    buf = kt % 2
                    if sbuf_last_exp[buf]:
                        twait(c, c.s_act, sbuf_last_exp[buf])
                    last = None
                    for s_ in range(2):
                        base = s_ * HD
                        lhsT = kt3[base:base + HD, hp, kt * P:(kt + 1) * P]
                        rhs = qt3[base:base + HD, hp, qb * 512:(qb + 1) * 512]
                        last = nc.tensor.matmul(
                            sp[buf][:, s_ * 512:(s_ + 1) * 512], lhsT, rhs,
                            start=True, stop=True, tile_position=(base, 0))
                    return c.s_pe.inc(last)

                def emit_exp(kt, sc_pe):
                    buf = kt % 2
                    eslot = kt % 4
                    swait(c, c.s_pe, sc_pe)
                    if eslot_last_av[eslot]:
                        swait(c, c.s_pe, eslot_last_av[eslot])
                    ins = nc.scalar.activation(
                        c.esb[:, eslot * D:(eslot + 1) * D], sp[buf][:, :], AF.Exp,
                        bias=_fa(-20.0), scale=_fa(0.125))
                    v = c.s_act.inc(ins)
                    sbuf_last_exp[buf] = v
                    return v

                def emit_av(kt, hp=hp, qb=qb):
                    twait(c, c.s_act, exp_of.get(kt + 1, exp_tail[0] or exp_of[kt]))
                    if kt == 0:
                        twait(c, c.s_dve, norm_done)
                    eslot = kt % 4
                    last = None
                    for s_ in range(2):
                        h = 2 * hp + s_
                        lhsT = vv4[:, kt, h, :]
                        rhs = c.esb[:, eslot * D + s_ * 512:
                                    eslot * D + (s_ + 1) * 512]
                        last = nc.tensor.matmul(
                            up[:, s_ * 512:(s_ + 1) * 512], lhsT, rhs,
                            start=(kt == 0), stop=(kt == NT - 1))
                    v = c.s_pe.inc(last)
                    eslot_last_av[eslot] = v
                    return v

                exp_tail = [0]
                for kt in range(NT):
                    sc_pe = emit_scores(kt)
                    exp_of[kt] = emit_exp(kt, sc_pe)
                    if kt == 2 and pend_norm is not None:
                        norm_done, chain_done = _attn_a_norm(c, *pend_norm, up, bc)
                        pend_norm = None
                    if kt >= AVLAG:
                        emit_av(kt - AVLAG)
                exp_tail[0] = c.s_act.inc(
                    nc.scalar.activation(c.rows[:, 4:5], c.rows[:, 0:1], AF.Square))
                for kt in range(NT - AVLAG, NT):
                    emit_av(kt)
                pend_norm = (hp, qb, c.s_pe.n)
        norm_done, chain_done = _attn_a_norm(c, *pend_norm, up, bc)

    if c.debug == "un" and b == 0:
        ywait(c, c.s_dve, c.s_dve.n)
        ywait(c, c.s_act, c.s_act.n)
        nc.sync.dma_start(c.dbg1, c.qt[:, :]).then_inc(c.d_st.h, 16)
        nc.sync.dma_start(c.dbg2, c.kt[:, :]).then_inc(c.d_st.h, 16)
        nc.sync.dma_start(c.dbg3, c.vv[:, :]).then_inc(c.d_st.h, 16)
        c.d_st.n += 48
        vwait(c, c.d_st, c.d_st.n)
        swait(c, c.d_st, c.d_st.n)
        twait(c, c.d_st, c.d_st.n)

    with nc.psum_tensor("poA%d" % b, [P, D], F32) as poA, \
            nc.psum_tensor("poB%d" % b, [P, D], F32) as poB:
        po = (poA, poB)
        po_evac = [0, 0]
        add_inc = {}
        for qt_ in range(NT):
            buf = qt_ % 2
            twait(c, c.d_w, woa_ld)
            if qt_ == 0:
                twait(c, c.s_dve, chain_done)
                twait(c, c.s_act, c.s_act.n)
            if po_evac[buf]:
                twait(c, c.s_dve, po_evac[buf])
            mm = None
            for hp in range(8):
                lhsT = qt3[:, hp, qt_ * P:(qt_ + 1) * P]
                for nn in range(2):
                    mm = nc.tensor.matmul(
                        po[buf][:, nn * 512:(nn + 1) * 512], lhsT,
                        c.wdq[:, hp * D + nn * 512: hp * D + (nn + 1) * 512],
                        start=(hp == 0), stop=(hp == 7))
            pe = c.s_pe.inc(mm)
            vwait(c, c.s_pe, pe)
            ins = nc.vector.tensor_tensor(c.resid[:, qt_ * D:(qt_ + 1) * D],
                                          c.resid[:, qt_ * D:(qt_ + 1) * D],
                                          po[buf][:, :], ALU.add)
            po_evac[buf] = add_inc[qt_] = c.s_dve.inc(ins)
            if qt_ >= 1:
                _ln_accum(c, qt_ - 1, dve_wait=add_inc[qt_ - 1])
        last = c.s_dve.inc(_gap(c, 2))
        _ln_accum(c, NT - 1, dve_wait=last)


def _attn_a_norm(c, hp, qb, pe_after_av, up, bc):
    nc = c.nc
    usc = c.nsc[:, 0:1024]
    vwait(c, c.s_pe, pe_after_av)
    _gap(c, 8)
    with nc.allow_low_precision(reason="unnorm attn out staged as bf16 operand"):
        nc.vector.tensor_copy(usc[0:VW, :], up[0:VW, :])
    norm_done = c.s_dve.inc(_gap(c, 3))
    twait(c, c.s_dve, norm_done)
    mm = None
    for nn in range(2):
        mm = nc.tensor.matmul(bc[:, nn * 512:(nn + 1) * 512],
                              c.ones[HD:HD + 1, 0:HD],
                              usc[HD:VW, nn * 512:(nn + 1) * 512],
                              start=True, stop=True)
    pe = c.s_pe.inc(mm)
    vwait(c, c.s_pe, pe)
    _gap(c, 3)
    with nc.allow_low_precision(reason="softmax denom recip as bf16 matmul operand"):
        nc.vector.reciprocal(c.bcsb[0:HD, 0:1024], bc[:, :])
    _gap(c, 2)
    qt3 = c.qt[:, :].rearrange("p (e s) -> p e s", e=8)
    for s_ in range(2):
        nc.vector.tensor_tensor(
            qt3[s_ * HD:(s_ + 1) * HD, hp, qb * 512:(qb + 1) * 512],
            usc[0:HD, s_ * 512:(s_ + 1) * 512],
            c.bcsb[0:HD, s_ * 512:(s_ + 1) * 512], ALU.mult)
    chain_done = c.s_dve.inc(_gap(c, 2))
    return norm_done, chain_done


def _ffn(c, b, fi, fuse_next=False):
    nc = c.nc
    _ln_finish(c)
    gwait(c, c.s_pe, c.s_pe.n)
    gwait(c, c.d_x, c.wdt_written)
    src = c.wdt_scr.rearrange("(j p) d -> p j d", p=P)
    dst = c.wdq[:, :].rearrange("p (j d) -> p j d", d=D)
    nc.gpsimd.dma_start(dst, src).then_inc(c.d_w.h, 16)
    c.d_w.n += 16
    wdt_ld = c.d_w.n
    ywait(c, c.s_dve, c.s_dve.n)
    ywait(c, c.s_pe, c.s_pe.n)
    nc.sync.dma_start(c.bias_row, c.bias).then_inc(c.d_x.h, 16)
    c.d_x.n += 16
    nc.sync.dma_start(c.onesf_sb, c.onesf).then_inc(c.d_x.h, 16)
    c.d_x.n += 16
    brow_ld = c.d_x.n
    with nc.psum_tensor("pbb%d%d" % (b, fi), [P, D], F32) as pb:
        twait(c, c.d_x, brow_ld)
        mm = None
        for nn in range(2):
            mm = nc.tensor.matmul(pb[:, nn * 512:(nn + 1) * 512], c.onesf_sb,
                                  c.bias_row[0:1, nn * 512:(nn + 1) * 512],
                                  start=True, stop=True)
        pe = c.s_pe.inc(mm)
        vwait(c, c.s_pe, pe)
        c.s_dve.inc(nc.vector.tensor_copy(c.bias_bc[:, :], pb[:, :]))

    with ExitStack() as stk:
        tp = TransPair(c, "tf%d%d" % (b, fi), stk)
        pf = [stk.enter_context(nc.psum_tensor("ffA%d%d" % (b, fi), [P, D], F32)),
              stk.enter_context(nc.psum_tensor("ffB%d%d" % (b, fi), [P, D], F32))]
        pf_evac = [0, 0]
        xns_free = [0, 0]
        xd_free = 0
        pend_add = None

        add_inc = {}

        def flush_add():
            nonlocal pend_add
            if pend_add is None:
                return
            tt, bbuf, ppe = pend_add
            pend_add = None
            vwait(c, c.s_pe, ppe)
            tmp = c.scr[:, 0:D]
            nc.vector.tensor_tensor(tmp, pf[bbuf][:, :], c.bias_bc[:, :], ALU.add)
            ins = nc.vector.tensor_tensor(c.resid[:, tt * D:(tt + 1) * D],
                                          c.resid[:, tt * D:(tt + 1) * D], tmp,
                                          ALU.add)
            pf_evac[bbuf] = add_inc[tt] = c.s_dve.inc(ins)
            if fuse_next and tt >= 1:
                _ln_accum(c, tt - 1, dve_wait=add_inc[tt - 1])

        for t in range(NT):
            xn_tile = c.xns[:, (t % 2) * D:(t % 2) * D + D]
            if xns_free[t % 2]:
                swait(c, c.s_dve, xns_free[t % 2])
            ac = _ln_apply(c, t, xn_tile)
            vwait(c, c.s_act, ac)
            if xd_free:
                vwait(c, c.s_pe, xd_free)
            dvq = _quant_tile(c, xn_tile, c.qzb[:, :], mid_fn=flush_add)
            xns_free[t % 2] = dvq
            if c.debug == "ffnq" and b == 0 and fi == 0 and t == 0:
                ywait(c, c.s_dve, c.s_dve.n)
                nc.sync.dma_start(c.dbg1[:, 0:1024], c.qzb[:, :]).then_inc(c.d_st.h, 16)
                nc.sync.dma_start(c.dbg1[:, 1024:3072], c.qm[:, :]).then_inc(c.d_st.h, 16)
                nc.sync.dma_start(c.dbg2.bitcast(F32)[:, 0:1024],
                                  xn_tile).then_inc(c.d_st.h, 16)
                nc.sync.dma_start(c.dbg6, c.qs[:, 0:96]).then_inc(c.d_st.h, 16)
                c.d_st.n += 64
                vwait(c, c.d_st, c.d_st.n)
                swait(c, c.d_st, c.d_st.n)
            tp.transpose8(c.qzb[:, :],
                          lambda j: c.xnt[:, j * 512:j * 512 + P],
                          pe_wait=(c.s_dve, dvq))
            xd_free = c.s_pe.n
            dvt = c.s_dve.n
            buf = t % 2
            twait(c, c.s_dve, dvt)
            if t == 0:
                twait(c, c.d_w, wdt_ld)
            if pf_evac[buf]:
                twait(c, c.s_dve, pf_evac[buf])
            mm = None
            for j in range(8):
                lhsT = c.xnt[:, j * 512:j * 512 + P]
                for nn in range(2):
                    mm = nc.tensor.matmul(
                        pf[buf][:, nn * 512:(nn + 1) * 512], lhsT,
                        c.wdq[:, j * D + nn * 512: j * D + (nn + 1) * 512],
                        start=(j == 0), stop=(j == 7))
            pe = c.s_pe.inc(mm)
            pend_add = (t, buf, pe)
        flush_add()
        if fuse_next:
            last = c.s_dve.inc(_gap(c, 2))
            _ln_accum(c, NT - 1, dve_wait=last)


def _attn_b(c, b):
    nc = c.nc
    QB = 256
    _ln_finish(c)
    wvt_ld = _wdq_load(c, c.wvt, c.s_pe.n)

    qt3 = c.qt[:, :].rearrange("p (j s) -> p j s", j=8)
    kt3 = c.kt[:, :].rearrange("p (j s) -> p j s", j=8)

    rope_done = 0
    pv_evac = [0, 0]
    xnb_free = [0, 0]
    for quar in range(NQ):
        with ExitStack() as stk:
            tp = TransPair(c, "tb%d%d" % (b, quar), stk)
            for t_rel in range(TPQ):
                t = quar * TPQ + t_rel
                xn_tile = c.xnb[:, (t % 2) * D:(t % 2) * D + D]
                if xnb_free[t % 2]:
                    swait(c, c.s_pe, xnb_free[t % 2])
                ac = _ln_apply(c, t, xn_tile)
                tp.transpose8(
                    xn_tile,
                    lambda j, t_rel=t_rel: c.xnt[:, j * 512 + t_rel * P:
                                                 j * 512 + t_rel * P + P],
                    pe_wait=(c.s_act, ac))
                xnb_free[t % 2] = c.s_pe.n
            dv_xnt = c.s_dve.inc(_gap(c, 2))
        with nc.psum_tensor("bqA%d%d" % (b, quar), [P, 512], F32) as psA0, \
                nc.psum_tensor("bqB%d%d" % (b, quar), [P, 512], F32) as psB0, \
                nc.psum_tensor("bqC%d%d" % (b, quar), [P, 512], F32) as psA1, \
                nc.psum_tensor("bqD%d%d" % (b, quar), [P, 512], F32) as psB1:
            pspair = ((psA0, psB0), (psA1, psB1))
            rope_of_set = [0, 0]
            cs_used = [0, 0]
            step = 0
            for jp in range(4):
                cslot = jp % 2
                if cs_used[cslot]:
                    gwait(c, c.s_dve, cs_used[cslot])
                gwait(c, c.s_act, c.ln_act_end)
                cssem = c.d_csl[cslot]
                nc.gpsimd.dma_start(
                    c.csb[:, cslot * 1024:cslot * 1024 + 512],
                    c.cst[jp * P:(jp + 1) * P, quar * 512:(quar + 1) * 512]
                ).then_inc(cssem.h, 16)
                cssem.n += 16
                nc.gpsimd.dma_start(
                    c.csb[:, cslot * 1024 + 512:cslot * 1024 + 1024],
                    c.cst[512 + jp * P: 512 + (jp + 1) * P,
                          quar * 512:(quar + 1) * 512]).then_inc(cssem.h, 16)
                cssem.n += 16
                cs_ld = (cssem, cssem.n)
                for wsrc, dst_sb in ((c.wqtp, qt3), (c.wktp, kt3)):
                    sidx = step % 2
                    psA, psB = pspair[sidx]
                    for which, ps in ((jp, psA), (jp + 4, psB)):
                        src = wsrc[:, which * P:(which + 1) * P].rearrange(
                            "(j p) e -> p j e", p=P)
                        slot_ap, wv, slot = _wload(c, src)
                        slot3 = slot_ap.rearrange("p (j e) -> p j e", e=P)
                        twait(c, wv[0], wv[1])
                        twait(c, c.s_dve, dv_xnt)
                        if rope_of_set[sidx]:
                            twait(c, c.s_dve, rope_of_set[sidx])
                        mm = None
                        for j in range(8):
                            mm = nc.tensor.matmul(
                                ps[:, :], slot3[:, j, :],
                                c.xnt[:, j * 512:(j + 1) * 512],
                                start=(j == 0), stop=(j == 7))
                        pe = c.s_pe.inc(mm)
                        c.wfree[slot] = pe
                    pe_pair = c.s_pe.n
                    vwait(c, c.s_pe, pe_pair)
                    vwait(c, cs_ld[0], cs_ld[1])
                    _gap(c, 3)
                    cos = c.csb[:, cslot * 1024:cslot * 1024 + 512]
                    sin = c.csb[:, cslot * 1024 + 512:cslot * 1024 + 1024]
                    t1 = c.scr[:, 0:512]
                    t2 = c.scr[:, 512:1024]
                    dcol = quar * 512
                    nc.vector.tensor_tensor(t1, psA[:, :], cos, ALU.mult)
                    nc.vector.tensor_tensor(t2, psB[:, :], sin, ALU.mult)
                    nc.vector.tensor_tensor(dst_sb[:, jp, dcol:dcol + 512],
                                            t1, t2, ALU.subtract)
                    nc.vector.tensor_tensor(t1, psA[:, :], sin, ALU.mult)
                    nc.vector.tensor_tensor(t2, psB[:, :], cos, ALU.mult)
                    ins = nc.vector.tensor_tensor(dst_sb[:, jp + 4, dcol:dcol + 512],
                                                  t1, t2, ALU.add)
                    rope_of_set[sidx] = rope_done = c.s_dve.inc(ins)
                    cs_used[cslot] = rope_done
                    step += 1
        with nc.psum_tensor("bvA%d%d" % (b, quar), [P, 512], F32) as pvA, \
                nc.psum_tensor("bvB%d%d" % (b, quar), [P, 512], F32) as pvB:
            pv = (pvA, pvB)
            buf = 0
            for t_rel in range(TPQ):
                t = quar * TPQ + t_rel
                lhsT3 = c.xnt[:, :].rearrange("p (j q) -> p j q", q=512)
                for half in range(2):
                    twait(c, c.d_w, wvt_ld)
                    if t_rel == 0 and half == 0:
                        twait(c, c.s_dve, rope_done)
                    if pv_evac[buf]:
                        twait(c, c.s_act, pv_evac[buf])
                    mm = None
                    for j in range(8):
                        mm = nc.tensor.matmul(
                            pv[buf][:, :], lhsT3[:, j, t_rel * P:(t_rel + 1) * P],
                            c.wdq[:, j * D + half * 512: j * D + (half + 1) * 512],
                            start=(j == 0), stop=(j == 7))
                    pe = c.s_pe.inc(mm)
                    swait(c, c.s_pe, pe)
                    pv_evac[buf] = c.s_act.inc(nc.scalar.activation(
                        c.vv[:, t * D + half * 512: t * D + (half + 1) * 512],
                        pv[buf][:, :], AF.Copy))
                    buf = (buf + 1) % 2
    dv_proj = c.s_dve.inc(_gap(c, 3))
    for _ in range(2):
        ac_bv = c.s_act.inc(nc.scalar.activation(c.rows[:, 4:5], c.rows[:, 0:1],
                                                 AF.Square))

    if c.debug == "projb" and b == 0:
        ywait(c, c.s_dve, c.s_dve.n)
        ywait(c, c.s_act, c.s_act.n)
        for dsrc, ddst in ((c.qt[:, :], c.dbg1), (c.kt[:, :], c.dbg2),
                           (c.vv[:, 0:NT * D], c.dbg3[:, 0:NT * D])):
            nc.sync.dma_start(ddst, dsrc).then_inc(c.d_st.h, 16)
            c.d_st.n += 16
        vwait(c, c.d_st, c.d_st.n)
        swait(c, c.d_st, c.d_st.n)
        twait(c, c.d_st, c.d_st.n)

    wot_ld = _wdq_load(c, c.wot, c.s_pe.n)

    with nc.psum_tensor("bsA%d" % b, [P, 512], F32) as sb0, \
            nc.psum_tensor("bsB%d" % b, [P, 512], F32) as sb1, \
            nc.psum_tensor("bo0%d" % b, [P, 512], F32) as ot0, \
            nc.psum_tensor("bo1%d" % b, [P, 512], F32) as ot1, \
            nc.psum_tensor("bo2%d" % b, [P, 512], F32) as ot2, \
            nc.psum_tensor("bo3%d" % b, [P, 512], F32) as ot3, \
            nc.psum_tensor("bdn%d" % b, [1, 512], F32) as den:
        ot = (ot0, ot1, ot2, ot3)
        sb = (sb0, sb1)
        PAIRLAG = 3
        sbuf_last_exp = [0, 0]
        eslot_last_av = [0] * 4
        norm_done = 0
        chain_done = 0
        pend_norm = None
        twait(c, c.s_dve, dv_proj)
        twait(c, c.s_act, ac_bv)
        for qb in range(8):
            exp_of = {}

            def b_scores(kt, qb=qb):
                pair = kt // 2
                buf = sb[pair % 2]
                if kt % 2 == 0 and sbuf_last_exp[pair % 2]:
                    twait(c, c.s_act, sbuf_last_exp[pair % 2])
                half_ = kt % 2
                mm = None
                for j in range(8):
                    lhsT = kt3[:, j, kt * P:(kt + 1) * P]
                    rhs = qt3[:, j, qb * QB:(qb + 1) * QB]
                    mm = nc.tensor.matmul(buf[:, half_ * QB:(half_ + 1) * QB],
                                          lhsT, rhs, start=(j == 0), stop=(j == 7))
                return c.s_pe.inc(mm)

            def b_exp(kt, sc_pe):
                pair = kt // 2
                eslot = pair % 4
                swait(c, c.s_pe, sc_pe)
                if eslot_last_av[eslot]:
                    swait(c, c.s_pe, eslot_last_av[eslot])
                ins = nc.scalar.activation(
                    c.esb[:, eslot * 512:(eslot + 1) * 512],
                    sb[pair % 2][:, :], AF.Exp, bias=_fa(-8.0), scale=_fa(1.0 / 32.0))
                v = c.s_act.inc(ins)
                sbuf_last_exp[pair % 2] = v
                return v

            def b_av(kt, qb=qb):
                pair = kt // 2
                eslot = pair % 4
                twait(c, c.s_act, exp_of.get(pair + 1, exp_tail[0] or exp_of[pair]))
                if kt == 0:
                    twait(c, c.s_dve, norm_done)
                rhs = c.esb[:, eslot * 512 + (kt % 2) * QB:
                            eslot * 512 + (kt % 2) * QB + QB]
                mm = None
                for dpair in range(4):
                    for s_ in range(2):
                        dt_ = dpair * 2 + s_
                        lhsT = c.vv[:, kt * D + dt_ * P: kt * D + dt_ * P + P]
                        mm = nc.tensor.matmul(
                            ot[dpair][:, s_ * QB:(s_ + 1) * QB], lhsT, rhs,
                            start=(kt == 0), stop=(kt == NT - 1))
                mm = nc.tensor.matmul(den[0:1, 0:QB], c.ones[:, 0:1], rhs,
                                      start=(kt == 0), stop=(kt == NT - 1))
                v = c.s_pe.inc(mm)
                eslot_last_av[eslot] = v
                return v

            exp_tail = [0]
            for kt in range(NT):
                sc_pe = b_scores(kt)
                if kt % 2 == 1:
                    exp_of[kt // 2] = b_exp(kt, sc_pe)
                if kt == 4 and pend_norm is not None:
                    norm_done, chain_done = _attn_b_norm(c, *pend_norm, ot, den)
                    pend_norm = None
                if kt >= 2 * PAIRLAG:
                    b_av(kt - 2 * PAIRLAG)
            exp_tail[0] = c.s_act.inc(
                nc.scalar.activation(c.rows[:, 4:5], c.rows[:, 0:1], AF.Square))
            for kt in range(NT - 2 * PAIRLAG, NT):
                b_av(kt)
            pend_norm = (qb, c.s_pe.n)
        norm_done, chain_done = _attn_b_norm(c, *pend_norm, ot, den)

    with nc.psum_tensor("bpA%d" % b, [P, D], F32) as poA, \
            nc.psum_tensor("bpB%d" % b, [P, D], F32) as poB:
        po = (poA, poB)
        po_evac = [0, 0]
        add_inc = {}
        for qt_ in range(NT):
            buf = qt_ % 2
            twait(c, c.d_w, wot_ld)
            if qt_ == 0:
                twait(c, c.s_dve, chain_done)
                twait(c, c.s_act, c.s_act.n)
            if po_evac[buf]:
                twait(c, c.s_dve, po_evac[buf])
            mm = None
            for j in range(8):
                lhsT = qt3[:, j, qt_ * P:(qt_ + 1) * P]
                for nn in range(2):
                    mm = nc.tensor.matmul(
                        po[buf][:, nn * 512:(nn + 1) * 512], lhsT,
                        c.wdq[:, j * D + nn * 512: j * D + (nn + 1) * 512],
                        start=(j == 0), stop=(j == 7))
            pe = c.s_pe.inc(mm)
            vwait(c, c.s_pe, pe)
            ins = nc.vector.tensor_tensor(c.resid[:, qt_ * D:(qt_ + 1) * D],
                                          c.resid[:, qt_ * D:(qt_ + 1) * D],
                                          po[buf][:, :], ALU.add)
            po_evac[buf] = add_inc[qt_] = c.s_dve.inc(ins)
            if qt_ >= 1:
                _ln_accum(c, qt_ - 1, dve_wait=add_inc[qt_ - 1])
        last = c.s_dve.inc(_gap(c, 2))
        _ln_accum(c, NT - 1, dve_wait=last)


def _attn_b_norm(c, qb, pe_after_av, ot, den):
    nc = c.nc
    QB = 256
    osc = c.nsc[:, 0:2048]
    vwait(c, c.s_pe, pe_after_av)
    _gap(c, 8)
    with nc.allow_low_precision(reason="unnorm attn out staged as bf16 operand"):
        for dpair in range(4):
            nc.vector.tensor_copy(osc[:, dpair * 512:(dpair + 1) * 512],
                                  ot[dpair][:, :])
    rr = c.rrow[0:1, 0:QB]
    with nc.allow_low_precision(reason="softmax denom staged bf16 for bcast matmul"):
        nc.vector.tensor_copy(rr, den[0:1, 0:QB])
    norm_done = c.s_dve.inc(_gap(c, 3))
    twait(c, c.s_dve, norm_done)
    c.norm_ctr += 1
    with nc.psum_tensor("bbc%d" % c.norm_ctr, [P, QB], F32) as bcb:
        mm = nc.tensor.matmul(bcb[0:HD, :], c.ones[0:1, 0:HD], rr,
                              start=True, stop=True)
        mm = nc.tensor.matmul(bcb[HD:P, :], c.ones[0:1, 0:HD], rr,
                              start=True, stop=True, tile_position=(0, 64))
        pe = c.s_pe.inc(mm)
        vwait(c, c.s_pe, pe)
        _gap(c, 3)
        with nc.allow_low_precision(reason="softmax denom recip as bf16 operand"):
            nc.vector.reciprocal(c.bcsb[:, 0:QB], bcb[:, :])
        _gap(c, 2)
    qt3 = c.qt[:, :].rearrange("p (j s) -> p j s", j=8)
    for dpair in range(4):
        for s_ in range(2):
            j = dpair * 2 + s_
            nc.vector.tensor_tensor(
                qt3[:, j, qb * QB:(qb + 1) * QB],
                osc[:, dpair * 512 + s_ * QB: dpair * 512 + (s_ + 1) * QB],
                c.bcsb[:, 0:QB], ALU.mult)
    chain_done = c.s_dve.inc(_gap(c, 2))
    return norm_done, chain_done


def _prep_inputs(inputs):
    cst, ident, ones_bf, ones_f32 = _host_constants()
    perm = np.concatenate([np.arange(0, D, 2), np.arange(1, D, 2)])
    com = {
        "wqa": np.ascontiguousarray(np.concatenate(list(inputs["L0_attn_a_W_query"]), axis=1)),
        "wka": np.ascontiguousarray(np.concatenate(list(inputs["L0_attn_a_W_key"]), axis=1)),
        "wva": np.ascontiguousarray(np.concatenate(list(inputs["L0_attn_a_W_val"]), axis=1)),
        "woa": np.ascontiguousarray(np.asarray(inputs["L0_attn_a_W_out"]).reshape(D, D)),
        "wqtp": np.ascontiguousarray(np.asarray(inputs["L0_attn_b_wq"]).T[:, perm]),
        "wktp": np.ascontiguousarray(np.asarray(inputs["L0_attn_b_wk"]).T[:, perm]),
        "wvt": np.ascontiguousarray(np.asarray(inputs["L0_attn_b_wv"]).T),
        "wot": np.ascontiguousarray(np.asarray(inputs["L0_attn_b_wo"]).T),
        "wd": np.ascontiguousarray(np.asarray(inputs["L0_ffn_weight"])),
        "bias": np.ascontiguousarray(np.asarray(inputs["L0_ffn_bias"]).reshape(1, D)),
        "cst": cst,
        "identc": ident,
        "onesb": ones_bf[:, 0:64].copy(),
        "onesf": ones_f32,
    }
    x = np.asarray(inputs["L0_attn_a_q"])
    in_maps = []
    for core in range(NC):
        m = dict(com)
        m["x"] = np.ascontiguousarray(x[NB * core:NB * (core + 1)].reshape(NB * S, D))
        in_maps.append(m)
    return in_maps


def kernel(**inputs):
    in_maps = _prep_inputs(inputs)
    nc = build_program(upto=4)
    res = run_bass_kernel_spmd(nc, in_maps, core_ids=list(range(NC)))
    out = np.empty((NC * NB * S, D), dtype=np.float32)
    for core in range(NC):
        out[core * NB * S:(core + 1) * NB * S] = res.results[core]["out"]
    return out.reshape(16, S, D)


# version 10 (final)
# speedup vs baseline: 1.0224x; 1.0224x over previous
from contextlib import ExitStack

import numpy as np
import ml_dtypes
import concourse.bass as bass
import concourse.mybir as mybir
from concourse.bass_utils import run_bass_kernel_spmd

F32 = mybir.dt.float32
BF16 = mybir.dt.bfloat16
I32 = mybir.dt.int32
AF = mybir.ActivationFunctionType
ALU = mybir.AluOpType
AX = mybir.AxisListType

P = 128
D = 1024
S = 2048
NT = S // P
NQ = 4
TPQ = 4
NB = 2
NH = 16
HD = 64
NC = 8
VW = HD + 1


def _f(x):
    return float(np.float32(x))


def _fa(x):
    return np.float32(x)


class Sem:
    _next_id = 0

    def __init__(self, handle):
        self.h = handle
        self.n = 0
        self.id = Sem._next_id
        Sem._next_id += 1

    def inc(self, instr, k=1):
        instr.then_inc(self.h, k)
        self.n += k
        return self.n


class WaitCache:

    def __init__(self, c):
        self.c = c
        self.seen = {}

    def wait(self, eng, eng_name, sem, val):
        if val <= 0:
            return
        key = (eng_name, sem.id)
        if self.seen.get(key, -1) >= val:
            return
        eng.wait_ge(sem.h, val)
        self.seen[key] = val


def twait(c, sem, val):
    c.wcache.wait(c.nc.tensor, "t", sem, val)


def vwait(c, sem, val):
    c.wcache.wait(c.nc.vector, "v", sem, val)


def swait(c, sem, val):
    c.wcache.wait(c.nc.scalar, "s", sem, val)


def gwait(c, sem, val):
    c.wcache.wait(c.nc.gpsimd, "g", sem, val)


def ywait(c, sem, val):
    c.wcache.wait(c.nc.sync, "y", sem, val)


class Ctx:
    pass


def _gap(c, n=3):
    ins = None
    for _ in range(n):
        ins = c.nc.vector.tensor_copy(c.rows[:, 6:7], c.rows[:, 7:8])
    return ins


def _host_constants():
    t = np.arange(S, dtype=np.float64)
    inv = 1.0 / (10000.0 ** (np.arange(0, D, 2, dtype=np.float64) / D))
    ang = np.outer(t, inv)
    cst = np.empty((D, S), dtype=np.float32)
    cst[: D // 2] = np.cos(ang).T
    cst[D // 2:] = np.sin(ang).T
    ident = np.eye(P, dtype=ml_dtypes.bfloat16)
    ones_bf = np.ones((P, P), dtype=ml_dtypes.bfloat16)
    ones_f32 = np.ones((1, P), dtype=np.float32)
    return cst, ident, ones_bf, ones_f32


def build_program(upto=4, debug=None):
    nc = bass.Bass("TRN2", target_bir_lowering=False)
    c = Ctx()
    c.nc = nc
    c.upto = upto
    c.debug = debug

    c.x = nc.dram_tensor("x", [NB * S, D], F32, kind="ExternalInput").ap()
    c.wqa = nc.dram_tensor("wqa", [D, D], F32, kind="ExternalInput").ap()
    c.wka = nc.dram_tensor("wka", [D, D], F32, kind="ExternalInput").ap()
    c.wva = nc.dram_tensor("wva", [D, D], F32, kind="ExternalInput").ap()
    c.woa = nc.dram_tensor("woa", [D, D], F32, kind="ExternalInput").ap()
    c.wqtp = nc.dram_tensor("wqtp", [D, D], F32, kind="ExternalInput").ap()
    c.wktp = nc.dram_tensor("wktp", [D, D], F32, kind="ExternalInput").ap()
    c.wvt = nc.dram_tensor("wvt", [D, D], F32, kind="ExternalInput").ap()
    c.wot = nc.dram_tensor("wot", [D, D], F32, kind="ExternalInput").ap()
    c.wd = nc.dram_tensor("wd", [D, D], F32, kind="ExternalInput").ap()
    c.bias = nc.dram_tensor("bias", [1, D], F32, kind="ExternalInput").ap()
    c.cst = nc.dram_tensor("cst", [D, S], F32, kind="ExternalInput").ap()
    c.identc = nc.dram_tensor("identc", [P, P], BF16, kind="ExternalInput").ap()
    c.onesb = nc.dram_tensor("onesb", [P, HD], BF16, kind="ExternalInput").ap()
    c.onesf = nc.dram_tensor("onesf", [1, P], F32, kind="ExternalInput").ap()
    c.out = nc.dram_tensor("out", [NB * S, D], F32, kind="ExternalOutput").ap()
    if debug:
        c.dbg1 = nc.dram_tensor("dbg1", [P, 8 * S], BF16, kind="ExternalOutput").ap()
        c.dbg2 = nc.dram_tensor("dbg2", [P, 8 * S], BF16, kind="ExternalOutput").ap()
        c.dbg3 = nc.dram_tensor("dbg3", [P, NT * NH * VW], BF16, kind="ExternalOutput").ap()
        c.dbg7 = nc.dram_tensor("dbg7", [P, 8 * S], BF16, kind="ExternalOutput").ap()
        c.dbg8 = nc.dram_tensor("dbg8", [P, 8 * S], BF16, kind="ExternalOutput").ap()
        c.dbg9 = nc.dram_tensor("dbg9", [P, NT * NH * VW], BF16, kind="ExternalOutput").ap()
        c.dbg4 = nc.dram_tensor("dbg4", [P, 4 * NT], F32, kind="ExternalOutput").ap()
        c.dbg5 = nc.dram_tensor("dbg5", [P, 4096], BF16, kind="ExternalOutput").ap()
        c.dbg6 = nc.dram_tensor("dbg6", [P, 96], F32, kind="ExternalOutput").ap()
    c.wdt_scr = nc.dram_tensor("wdt_scr", [D, D], BF16, kind="Internal").ap()

    with ExitStack() as st:
        E = st.enter_context
        c.resid = E(nc.sbuf_tensor("resid", [P, NT * D], F32))
        c.qt = E(nc.sbuf_tensor("qt", [P, 8 * S], BF16))
        c.kt = E(nc.sbuf_tensor("kt", [P, 8 * S], BF16))
        c.vv = E(nc.sbuf_tensor("vv", [P, NT * NH * VW], BF16))
        c.xnt = E(nc.sbuf_tensor("xnt", [P, 8 * 512], BF16))
        c.wdq = E(nc.sbuf_tensor("wdq", [P, 8 * D], BF16))
        c.wbuf = E(nc.sbuf_tensor("wbuf", [P, 3 * D], BF16))
        c.scr = E(nc.sbuf_tensor("scr", [P, 2 * D], F32))
        c.esb = E(nc.sbuf_tensor("esb", [P, 4 * D], BF16))
        c.ident = E(nc.sbuf_tensor("ident", [P, P], BF16))
        c.ones = E(nc.sbuf_tensor("ones", [P, HD], BF16))
        c.stats = E(nc.sbuf_tensor("stats", [P, 4 * NT], F32))
        c.rows = E(nc.sbuf_tensor("rows", [P, 8], F32))
        c.srows = E(nc.sbuf_tensor("srows", [P, 8], F32))
        c.qs = E(nc.sbuf_tensor("qs", [P, 160], F32))
        vvap = c.vv[:, :]
        c.xns = vvap[:, 0:4096].bitcast(F32)
        c.qv = vvap[:, 4096:6144].bitcast(F32)
        c.qm = vvap[:, 6144:8192]
        c.bias_bc = vvap[:, 8192:10240].bitcast(F32)
        c.qzb = vvap[:, 10240:11264]
        c.csb = c.esb[:, 0:2048]
        c.xnb = c.esb[:, 2048:4096]
        c.sqscr = c.esb[:, 0:2048].bitcast(F32)
        c.bcsb = c.xnt[:, 0:1024]
        c.rrow = c.xnt[0:1, 1024:2048]
        c.nsc = c.scr[:, 1024:2048].bitcast(BF16)
        c.rrow32 = c.xnt[0:1, 2048:4096].bitcast(F32)
        c.onesf_sb = c.scr[0:1, 1536:1664]
        c.bias_row = c.scr[0:1, 256:1280]

        c.s_pe = Sem(E(nc.semaphore("s_pe")))
        c.s_act = Sem(E(nc.semaphore("s_act")))
        c.s_dve = Sem(E(nc.semaphore("s_dve")))
        c.d_w = Sem(E(nc.semaphore("d_w")))
        c.d_x = Sem(E(nc.semaphore("d_x")))
        c.d_st = Sem(E(nc.semaphore("d_st")))
        c.d_wsl = [Sem(E(nc.semaphore("d_ws%d" % i))) for i in range(3)]
        c.d_csl = [Sem(E(nc.semaphore("d_cs%d" % i))) for i in range(2)]

        c.wcache = WaitCache(c)
        c.wslot = 0
        c.wfree = {}
        c.store_done = 0
        c.norm_ctr = 0
        _emit(c)
    return nc


def _emit(c):
    nc = c.nc
    for ap_sb, ap_hbm in ((c.ident[:, :], c.identc), (c.ones[:, :], c.onesb)):
        nc.sync.dma_start(ap_sb, ap_hbm).then_inc(c.d_x.h, 16)
        c.d_x.n += 16
    c.const_ld = c.d_x.n
    nc.vector.memset(c.rows[:, :], 0.0)
    ini = c.s_dve.inc(nc.vector.memset(c.srows[:, :], 0.0))
    swait(c, c.s_dve, ini)
    twait(c, c.s_dve, ini)
    gwait(c, c.s_dve, ini)
    ywait(c, c.s_dve, ini)

    _ffn_weight_prep(c)

    for b in range(NB):
        _load_x(c, b)
        if c.upto == -2:
            _ln_stats(c, b)
            c._skip_finish = True
            _ffn(c, b, 0)
            c._skip_finish = False
            _store_x(c, b)
            continue
        _attn_a(c, b)
        if c.upto >= 2:
            _ffn(c, b, 0, fuse_next=(c.upto >= 3))
        if c.upto >= 3:
            _attn_b(c, b)
        if c.upto >= 4:
            _ffn(c, b, 1)
        _store_x(c, b)
    ywait(c, c.d_st, c.d_st.n)


def _load_x(c, b):
    nc = c.nc
    if b > 0:
        ywait(c, c.d_st, c.store_done)
    src = c.x[b * S:(b + 1) * S, :].rearrange("(t p) d -> p t d", p=P)
    dst = c.resid[:, :].rearrange("p (t d) -> p t d", d=D)
    nc.sync.dma_start(dst, src).then_inc(c.d_x.h, 16)
    c.d_x.n += 16
    c.x_ready = c.d_x.n


def _store_x(c, b):
    nc = c.nc
    dst = c.out[b * S:(b + 1) * S, :].rearrange("(t p) d -> p t d", p=P)
    src = c.resid[:, :].rearrange("p (t d) -> p t d", d=D)
    ywait(c, c.s_dve, c.s_dve.n)
    nc.sync.dma_start(dst, src).then_inc(c.d_st.h, 16)
    c.d_st.n += 16
    c.store_done = c.d_st.n


def _wload(c, src_ap, ncols=D):
    nc = c.nc
    slot = c.wslot % 3
    c.wslot += 1
    free_at = c.wfree.get(slot)
    if free_at is not None:
        gwait(c, c.s_pe, free_at)
    dst = c.wbuf[:, slot * D: slot * D + ncols]
    sem = c.d_wsl[slot]
    nc.gpsimd.dma_start(dst, src_ap).then_inc(sem.h, 16)
    sem.n += 16
    return dst, (sem, sem.n), slot


def _wdq_load(c, wsrc, pe_free, dma_wait=None):
    nc = c.nc
    gwait(c, c.s_pe, pe_free)
    if dma_wait is not None:
        gwait(c, c.d_x, dma_wait)
    src = wsrc.rearrange("(j p) e -> p j e", p=P)
    dst = c.wdq[:, :].rearrange("p (j e) -> p j e", e=D)
    nc.gpsimd.dma_start(dst, src).then_inc(c.d_w.h, 16)
    c.d_w.n += 16
    return c.d_w.n


def _ffn_weight_prep(c):
    nc = c.nc
    prev_evac = 0
    for t in range(8):
        ywait(c, c.d_x, c.const_ld)
        if prev_evac:
            ywait(c, c.s_dve, c.s_dve.n)
        nc.sync.dma_start(c.scr[:, 0:D], c.wd[t * P:(t + 1) * P, :]).then_inc(c.d_x.h, 16)
        c.d_x.n += 16
        vwait(c, c.d_x, c.d_x.n)
        _quant_tile(c, c.scr[:, 0:D], c.qzb[:, :], m2_buf=c.bias_bc[:, 0:D])
        dv = c.s_dve.n
        with nc.psum_tensor("fwA%d" % t, [P, 4 * P], BF16) as pa, \
                nc.psum_tensor("fwB%d" % t, [P, 4 * P], BF16) as pbx:
            for g in range(2):
                ps = (pa, pbx)[g]
                twait(c, c.s_dve, dv)
                if prev_evac:
                    twait(c, c.s_dve, prev_evac)
                mm = None
                for j4 in range(4):
                    j = g * 4 + j4
                    mm = nc.tensor.transpose(ps[:, j4 * P:(j4 + 1) * P],
                                             c.qzb[:, j * P:(j + 1) * P],
                                             c.ident[:, :])
                pe = c.s_pe.inc(mm)
                vwait(c, c.s_pe, pe)
                dst = c.wdq[:, :].rearrange("p (j d) -> p j d", d=D)[
                    :, g * 4:(g + 1) * 4, t * P:(t + 1) * P]
                src = ps[:, :].rearrange("p (j q) -> p j q", q=P)
                prev_evac = c.s_dve.inc(nc.vector.tensor_copy(dst, src))
    ywait(c, c.s_dve, c.s_dve.n)
    src = c.wdq[:, :].rearrange("p (j d) -> p j d", d=D)
    dst = c.wdt_scr.rearrange("(j p) d -> p j d", p=P)
    nc.sync.dma_start(dst, src).then_inc(c.d_x.h, 16)
    c.d_x.n += 16
    c.wdt_written = c.d_x.n


def _quant_tile(c, xin, xout, mid_fn=None, m2_buf=None):
    nc = c.nc
    v3 = xin.rearrange("p (b k) -> p b k", k=32)
    am = c.qs[:, 0:32]
    sc = c.qs[:, 32:64]
    rs = c.qs[:, 64:96]
    nc.vector.tensor_reduce(am, v3, AX.X, ALU.max, apply_absolute_value=True)
    _gap(c)
    a7 = float(np.float32(1.0 / 7.5))
    b7 = float(np.float32(np.float64(1.0) / 7.5 - np.float64(np.float32(1.0 / 7.5))))
    sc2 = c.qs[:, 3 * 32:4 * 32]
    nc.vector.tensor_scalar(sc2, am, _f(a7), None, ALU.mult)
    _gap(c)
    nc.vector.scalar_tensor_tensor(sc2, am, _f(b7), sc2, ALU.mult, ALU.add)
    _gap(c)
    nc.vector.tensor_scalar(sc, sc2, _f(1e-30), None, ALU.max)
    _gap(c)
    nc.vector.reciprocal(rs, sc)
    _gap(c)
    rs_b = rs.unsqueeze(2).broadcast_to([P, 32, 32])
    sc_b = sc.unsqueeze(2).broadcast_to([P, 32, 32])
    qv3 = c.qv[:, :].rearrange("p (b k) -> p b k", k=32)
    nc.vector.tensor_tensor(qv3, v3, rs_b, ALU.mult)
    m1 = c.qm[:, 0:D]
    m2 = c.qm[:, D:2 * D]
    kk = c.qzb[:, :]
    av = c.nsc[:, 0:1024]
    nc.vector.scalar_tensor_tensor(av, c.qv[:, :], _f(-1.0), c.qv[:, :],
                                   ALU.mult, ALU.max)
    if mid_fn is not None:
        mid_fn()
    nc.vector.tensor_scalar(m1, av, _f(2.0), _f(4.0), ALU.is_ge, ALU.mult)
    nc.vector.tensor_scalar(m2, av, _f(4.0), _f(2.0), ALU.is_ge, ALU.mult)
    nc.vector.tensor_tensor(kk, m1, m2, ALU.add)
    nc.vector.tensor_scalar(kk, kk, _f(-1.0), _f(8.0), ALU.mult, ALU.add)
    yy = c.scr[:, D:2 * D]
    nc.vector.tensor_tensor(yy, c.qv[:, :], kk, ALU.mult)
    zi = c.scr[:, 0:D].bitcast(I32)
    nc.vector.tensor_copy(zi, yy)
    nc.vector.tensor_copy(c.qzb[:, :], zi)
    sg = m1
    nc.vector.tensor_scalar(sg, m1, _f(0.03125), _f(0.125), ALU.mult, ALU.add)
    nc.vector.scalar_tensor_tensor(sg, m2, _f(0.125), sg, ALU.mult, ALU.add)
    nc.vector.tensor_tensor(c.qzb[:, :], c.qzb[:, :], sg, ALU.mult)
    q3 = c.qzb[:, :].rearrange("p (b k) -> p b k", k=32)
    o3 = xout.rearrange("p (b k) -> p b k", k=32)
    nc.vector.tensor_tensor(o3, q3, sc_b, ALU.mult)
    return c.s_dve.inc(_gap(c, 3))


def _ln_accum(c, t, dve_wait=None):
    nc = c.nc
    tile = c.resid[:, t * D:(t + 1) * D]
    i1 = nc.vector.tensor_scalar(c.scr[:, 0:D], tile, _f(1.0), _f(0.0),
                                 ALU.mult, ALU.add,
                                 accum_out=c.stats[:, t:t + 1])
    c.s_dve.inc(i1)
    if dve_wait:
        swait(c, c.s_dve, dve_wait)
    i2 = nc.scalar.activation(c.sqscr[:, 0:D], tile, AF.Square,
                              accum_out=c.stats[:, NT + t:NT + t + 1])
    c.s_act.inc(i2)


def _ln_finish(c):
    nc = c.nc
    if getattr(c, "_skip_finish", False):
        return
    ac_sq = c.s_act.inc(nc.scalar.activation(c.srows[:, 4:5], c.srows[:, 0:1],
                                             AF.Square))
    mean = c.stats[:, 2 * NT:3 * NT]
    rstd = c.stats[:, 3 * NT:4 * NT]
    tmp = c.qs[:, 0:NT]
    vwait(c, c.s_act, ac_sq)
    _gap(c, 2)
    nc.vector.tensor_scalar(mean, c.stats[:, 0:NT], _f(1.0 / D), None, ALU.mult)
    _gap(c)
    nc.vector.tensor_tensor(tmp, mean, mean, ALU.mult)
    _gap(c)
    var = c.qs[:, 2 * NT:3 * NT]
    nc.vector.scalar_tensor_tensor(var, c.stats[:, NT:2 * NT], _f(1.0 / D), tmp,
                                   ALU.mult, ALU.subtract)
    dv_var = c.s_dve.inc(_gap(c, 3))
    swait(c, c.s_dve, dv_var)
    nc.scalar.activation(c.qs[:, NT:2 * NT], var, AF.Sqrt, bias=_fa(1e-5))
    ac_sq2 = c.s_act.inc(nc.scalar.activation(c.srows[:, 4:5], c.srows[:, 0:1],
                                              AF.Square))
    vwait(c, c.s_act, ac_sq2)
    nc.vector.reciprocal(rstd, c.qs[:, NT:2 * NT])
    _gap(c)
    nc.vector.scalar_tensor_tensor(c.stats[:, 0:NT], mean, _f(-1.0), rstd,
                                   ALU.mult, ALU.mult)
    dv_nb = c.s_dve.inc(_gap(c, 3))
    c.ln_ready = dv_nb
    c.ln_act_end = c.s_act.n


def _ln_stats(c, b):
    nc = c.nc
    vwait(c, c.d_x, c.x_ready)
    swait(c, c.d_x, c.x_ready)
    swait(c, c.s_dve, c.s_dve.n)
    for t in range(NT):
        _ln_accum(c, t)
    _ln_finish(c)


def _ln_apply(c, t, out_ap):
    nc = c.nc
    swait(c, c.s_dve, c.ln_ready)
    nc.scalar.activation(out_ap, c.resid[:, t * D:(t + 1) * D], AF.Identity,
                         bias=c.stats[:, t:t + 1],
                         scale=c.stats[:, 3 * NT + t:3 * NT + t + 1])
    ins = nc.scalar.activation(c.srows[:, 4:5], c.srows[:, 0:1], AF.Square)
    return c.s_act.inc(ins)


class TransPair:
    def __init__(self, c, name, stack):
        self.c = c
        self.ps = [stack.enter_context(c.nc.psum_tensor(name + "A", [P, 4 * P], BF16)),
                   stack.enter_context(c.nc.psum_tensor(name + "B", [P, 4 * P], BF16))]
        self.last_evac = [c.s_dve.n, c.s_dve.n]
        self.i = 0

    def transpose8(self, src_bf16, dst_fn, pe_wait=None):
        c, nc = self.c, self.c.nc
        for g in range(2):
            ps = self.ps[self.i % 2]
            if self.last_evac[self.i % 2]:
                twait(c, c.s_dve, self.last_evac[self.i % 2])
            if pe_wait is not None:
                twait(c, *pe_wait)
                pe_wait = None
            mm = None
            for j4 in range(4):
                j = g * 4 + j4
                mm = nc.tensor.transpose(ps[:, j4 * P:(j4 + 1) * P],
                                         src_bf16[:, j * P:(j + 1) * P],
                                         c.ident[:, :])
            pe = c.s_pe.inc(mm)
            vwait(c, c.s_pe, pe)
            ins = None
            for j4 in range(4):
                j = g * 4 + j4
                ins = nc.vector.tensor_copy(dst_fn(j), ps[:, j4 * P:(j4 + 1) * P])
            self.last_evac[self.i % 2] = c.s_dve.inc(ins)
            self.i += 1
        return c.s_dve.n


def _attn_a(c, b):
    nc = c.nc
    _ln_stats(c, b)
    if c.debug == "proj" and b == 0:
        ywait(c, c.s_dve, c.s_dve.n)
        nc.sync.dma_start(c.dbg6, c.qs[:, 0:96]).then_inc(c.d_st.h, 16)
        c.d_st.n += 16
    wva_ld = _wdq_load(c, c.wva, c.s_pe.n, dma_wait=c.wdt_written)

    qt3 = c.qt[:, :].rearrange("p (e s) -> p e s", e=8)
    kt3 = c.kt[:, :].rearrange("p (e s) -> p e s", e=8)
    vv4 = c.vv[:, :].rearrange("p (t h v) -> p t h v", t=NT, h=NH)

    with ExitStack() as stk:
        tp = TransPair(c, "ta%d" % b, stk)
        pq = [stk.enter_context(nc.psum_tensor("qkA%d" % b, [P, 512], F32)),
              stk.enter_context(nc.psum_tensor("qkB%d" % b, [P, 512], F32))]
        pq_evac = [0, 0]
        xnb_free = [0, 0]
        for quar in range(NQ):
            for t_rel in range(TPQ):
                t = quar * TPQ + t_rel
                xn_tile = c.xnb[:, (t % 2) * D:(t % 2) * D + D]
                if xnb_free[t % 2]:
                    swait(c, c.s_pe, xnb_free[t % 2])
                ac = _ln_apply(c, t, xn_tile)
                tp.transpose8(
                    xn_tile,
                    lambda j, t_rel=t_rel: c.xnt[:, j * 512 + t_rel * P:
                                                 j * 512 + t_rel * P + P],
                    pe_wait=(c.s_act, ac))
                xnb_free[t % 2] = c.s_pe.n
            dv_xnt = c.s_dve.inc(_gap(c, 2))
            if c.debug == "proj" and b == 0 and quar == 0:
                ywait(c, c.s_dve, dv_xnt)
                nc.sync.dma_start(c.dbg5, c.xnt[:, :]).then_inc(c.d_st.h, 16)
                c.d_st.n += 16
            for proj, wsrc, dst_sb in ((0, c.wqa, c.qt), (1, c.wka, c.kt)):
                for et in range(8):
                    src = wsrc[:, et * P:(et + 1) * P].rearrange(
                        "(j p) e -> p j e", p=P)
                    slot_ap, wv, slot = _wload(c, src)
                    slot3 = slot_ap.rearrange("p (j e) -> p j e", e=P)
                    buf = et % 2
                    twait(c, wv[0], wv[1])
                    twait(c, c.s_dve, dv_xnt)
                    if pq_evac[buf]:
                        twait(c, c.s_act, pq_evac[buf])
                    mm = None
                    for j in range(8):
                        mm = nc.tensor.matmul(
                            pq[buf][:, :], slot3[:, j, :],
                            c.xnt[:, j * 512:(j + 1) * 512],
                            start=(j == 0), stop=(j == 7))
                    pe = c.s_pe.inc(mm)
                    c.wfree[slot] = pe
                    swait(c, c.s_pe, pe)
                    ins = nc.scalar.activation(
                        dst_sb[:, et * S + quar * 512: et * S + quar * 512 + 512],
                        pq[buf][:, :], AF.Copy)
                    pq_evac[buf] = c.s_act.inc(ins)
            buf = 0
            for t_rel in range(TPQ):
                t = quar * TPQ + t_rel
                lhsT3 = c.xnt[:, :].rearrange("p (j q) -> p j q", q=512)
                for half in range(2):
                    twait(c, c.d_w, wva_ld)
                    if pq_evac[buf]:
                        twait(c, c.s_act, pq_evac[buf])
                    mm = None
                    for j in range(8):
                        mm = nc.tensor.matmul(
                            pq[buf][:, :], lhsT3[:, j, t_rel * P:(t_rel + 1) * P],
                            c.wdq[:, j * D + half * 512: j * D + (half + 1) * 512],
                            start=(j == 0), stop=(j == 7))
                    pe = c.s_pe.inc(mm)
                    swait(c, c.s_pe, pe)
                    dst = vv4[:, t, half * 8:(half + 1) * 8, 0:HD]
                    src3 = pq[buf][:, :].rearrange("p (h v) -> p h v", h=8)
                    pq_evac[buf] = c.s_act.inc(
                        nc.scalar.activation(dst, src3, AF.Copy))
                    buf = (buf + 1) % 2
    vones = vv4[:, :, :, HD:VW]
    nc.vector.memset(vones, 1.0)
    dv_v = c.s_dve.inc(_gap(c, 3))
    for _ in range(2):
        ac_v = c.s_act.inc(nc.scalar.activation(c.srows[:, 4:5], c.srows[:, 0:1],
                                                AF.Square))

    if c.debug == "proj":
        ywait(c, c.s_dve, c.s_dve.n)
        ywait(c, c.s_act, c.s_act.n)
        dsts = ((c.dbg1, c.dbg2, c.dbg3) if b == 0 else (c.dbg7, c.dbg8, c.dbg9))
        for dsrc, ddst in ((c.qt[:, :], dsts[0]), (c.kt[:, :], dsts[1]),
                           (c.vv[:, :], dsts[2])):
            nc.sync.dma_start(ddst, dsrc).then_inc(c.d_st.h, 16)
            c.d_st.n += 16
        vwait(c, c.d_st, c.d_st.n)
        swait(c, c.d_st, c.d_st.n)
        twait(c, c.d_st, c.d_st.n)

    woa_ld = _wdq_load(c, c.woa, c.s_pe.n)

    with nc.psum_tensor("spA%d" % b, [P, 1024], F32) as sp0, \
            nc.psum_tensor("spB%d" % b, [P, 1024], F32) as sp1, \
            nc.psum_tensor("upp%d" % b, [VW, 1024], F32) as up, \
            nc.psum_tensor("bcp%d" % b, [HD, 1024], F32) as bc:
        sp = (sp0, sp1)
        AVLAG = 3
        sbuf_last_exp = [0, 0]
        eslot_last_av = [0] * 4
        norm_done = 0
        chain_done = 0
        pend_norm = None
        twait(c, c.s_dve, dv_v)
        twait(c, c.s_act, ac_v)
        for hp in range(8):
            for qb in range(4):
                exp_of = {}

                def emit_scores(kt, hp=hp, qb=qb):
                    buf = kt % 2
                    if sbuf_last_exp[buf]:
                        twait(c, c.s_act, sbuf_last_exp[buf])
                    last = None
                    for s_ in range(2):
                        base = s_ * HD
                        lhsT = kt3[base:base + HD, hp, kt * P:(kt + 1) * P]
                        rhs = qt3[base:base + HD, hp, qb * 512:(qb + 1) * 512]
                        last = nc.tensor.matmul(
                            sp[buf][:, s_ * 512:(s_ + 1) * 512], lhsT, rhs,
                            start=True, stop=True, tile_position=(base, 0))
                    return c.s_pe.inc(last)

                def emit_exp(kt, sc_pe):
                    buf = kt % 2
                    eslot = kt % 4
                    swait(c, c.s_pe, sc_pe)
                    if eslot_last_av[eslot]:
                        swait(c, c.s_pe, eslot_last_av[eslot])
                    ins = nc.scalar.activation(
                        c.esb[:, eslot * D:(eslot + 1) * D], sp[buf][:, :], AF.Exp,
                        bias=_fa(-20.0), scale=_fa(0.125))
                    v = c.s_act.inc(ins)
                    sbuf_last_exp[buf] = v
                    return v

                def emit_av(kt, hp=hp, qb=qb):
                    twait(c, c.s_act, exp_of.get(kt + 1, exp_tail[0] or exp_of[kt]))
                    if kt == 0:
                        twait(c, c.s_dve, norm_done)
                    eslot = kt % 4
                    last = None
                    for s_ in range(2):
                        h = 2 * hp + s_
                        lhsT = vv4[:, kt, h, :]
                        rhs = c.esb[:, eslot * D + s_ * 512:
                                    eslot * D + (s_ + 1) * 512]
                        last = nc.tensor.matmul(
                            up[:, s_ * 512:(s_ + 1) * 512], lhsT, rhs,
                            start=(kt == 0), stop=(kt == NT - 1))
                    v = c.s_pe.inc(last)
                    eslot_last_av[eslot] = v
                    return v

                exp_tail = [0]
                for kt in range(NT):
                    sc_pe = emit_scores(kt)
                    exp_of[kt] = emit_exp(kt, sc_pe)
                    if kt == 2 and pend_norm is not None:
                        norm_done, chain_done = _attn_a_norm(c, *pend_norm, up, bc)
                        pend_norm = None
                    if kt >= AVLAG:
                        emit_av(kt - AVLAG)
                exp_tail[0] = c.s_act.inc(
                    nc.scalar.activation(c.srows[:, 4:5], c.srows[:, 0:1], AF.Square))
                for kt in range(NT - AVLAG, NT):
                    emit_av(kt)
                pend_norm = (hp, qb, c.s_pe.n)
        norm_done, chain_done = _attn_a_norm(c, *pend_norm, up, bc)

    if c.debug == "un" and b == 0:
        ywait(c, c.s_dve, c.s_dve.n)
        ywait(c, c.s_act, c.s_act.n)
        nc.sync.dma_start(c.dbg1, c.qt[:, :]).then_inc(c.d_st.h, 16)
        nc.sync.dma_start(c.dbg2, c.kt[:, :]).then_inc(c.d_st.h, 16)
        nc.sync.dma_start(c.dbg3, c.vv[:, :]).then_inc(c.d_st.h, 16)
        c.d_st.n += 48
        vwait(c, c.d_st, c.d_st.n)
        swait(c, c.d_st, c.d_st.n)
        twait(c, c.d_st, c.d_st.n)

    with nc.psum_tensor("poA%d" % b, [P, D], F32) as poA, \
            nc.psum_tensor("poB%d" % b, [P, D], F32) as poB:
        po = (poA, poB)
        po_evac = [0, 0]
        add_inc = {}
        for qt_ in range(NT):
            buf = qt_ % 2
            twait(c, c.d_w, woa_ld)
            if qt_ == 0:
                twait(c, c.s_dve, chain_done)
                twait(c, c.s_act, c.s_act.n)
            if po_evac[buf]:
                twait(c, c.s_dve, po_evac[buf])
            mm = None
            for hp in range(8):
                lhsT = qt3[:, hp, qt_ * P:(qt_ + 1) * P]
                for nn in range(2):
                    mm = nc.tensor.matmul(
                        po[buf][:, nn * 512:(nn + 1) * 512], lhsT,
                        c.wdq[:, hp * D + nn * 512: hp * D + (nn + 1) * 512],
                        start=(hp == 0), stop=(hp == 7))
            pe = c.s_pe.inc(mm)
            vwait(c, c.s_pe, pe)
            ins = nc.vector.tensor_tensor(c.resid[:, qt_ * D:(qt_ + 1) * D],
                                          c.resid[:, qt_ * D:(qt_ + 1) * D],
                                          po[buf][:, :], ALU.add)
            po_evac[buf] = add_inc[qt_] = c.s_dve.inc(ins)
            if qt_ >= 1:
                _ln_accum(c, qt_ - 1, dve_wait=add_inc[qt_ - 1])
        last = c.s_dve.inc(_gap(c, 2))
        _ln_accum(c, NT - 1, dve_wait=last)


def _attn_a_norm(c, hp, qb, pe_after_av, up, bc):
    nc = c.nc
    usc = c.nsc[:, 0:1024]
    vwait(c, c.s_pe, pe_after_av)
    _gap(c, 8)
    with nc.allow_low_precision(reason="unnorm attn out staged as bf16 operand"):
        nc.vector.tensor_copy(usc[0:VW, :], up[0:VW, :])
    norm_done = c.s_dve.inc(_gap(c, 3))
    twait(c, c.s_dve, norm_done)
    mm = None
    for nn in range(2):
        mm = nc.tensor.matmul(bc[:, nn * 512:(nn + 1) * 512],
                              c.ones[HD:HD + 1, 0:HD],
                              usc[HD:VW, nn * 512:(nn + 1) * 512],
                              start=True, stop=True)
    pe = c.s_pe.inc(mm)
    vwait(c, c.s_pe, pe)
    _gap(c, 3)
    with nc.allow_low_precision(reason="softmax denom recip as bf16 matmul operand"):
        nc.vector.reciprocal(c.bcsb[0:HD, 0:1024], bc[:, :])
    _gap(c, 2)
    qt3 = c.qt[:, :].rearrange("p (e s) -> p e s", e=8)
    for s_ in range(2):
        nc.vector.tensor_tensor(
            qt3[s_ * HD:(s_ + 1) * HD, hp, qb * 512:(qb + 1) * 512],
            usc[0:HD, s_ * 512:(s_ + 1) * 512],
            c.bcsb[0:HD, s_ * 512:(s_ + 1) * 512], ALU.mult)
    chain_done = c.s_dve.inc(_gap(c, 2))
    return norm_done, chain_done


def _ffn(c, b, fi, fuse_next=False):
    nc = c.nc
    _ln_finish(c)
    gwait(c, c.s_pe, c.s_pe.n)
    gwait(c, c.d_x, c.wdt_written)
    src = c.wdt_scr.rearrange("(j p) d -> p j d", p=P)
    dst = c.wdq[:, :].rearrange("p (j d) -> p j d", d=D)
    nc.gpsimd.dma_start(dst, src).then_inc(c.d_w.h, 16)
    c.d_w.n += 16
    wdt_ld = c.d_w.n
    ywait(c, c.s_dve, c.s_dve.n)
    ywait(c, c.s_pe, c.s_pe.n)
    nc.sync.dma_start(c.bias_row, c.bias).then_inc(c.d_x.h, 16)
    c.d_x.n += 16
    nc.sync.dma_start(c.onesf_sb, c.onesf).then_inc(c.d_x.h, 16)
    c.d_x.n += 16
    brow_ld = c.d_x.n
    with nc.psum_tensor("pbb%d%d" % (b, fi), [P, D], F32) as pb:
        twait(c, c.d_x, brow_ld)
        mm = None
        for nn in range(2):
            mm = nc.tensor.matmul(pb[:, nn * 512:(nn + 1) * 512], c.onesf_sb,
                                  c.bias_row[0:1, nn * 512:(nn + 1) * 512],
                                  start=True, stop=True)
        pe = c.s_pe.inc(mm)
        vwait(c, c.s_pe, pe)
        c.s_dve.inc(nc.vector.tensor_copy(c.bias_bc[:, :], pb[:, :]))

    with ExitStack() as stk:
        tp = TransPair(c, "tf%d%d" % (b, fi), stk)
        pf = [stk.enter_context(nc.psum_tensor("ffA%d%d" % (b, fi), [P, D], F32)),
              stk.enter_context(nc.psum_tensor("ffB%d%d" % (b, fi), [P, D], F32))]
        pf_evac = [0, 0]
        xns_free = [0, 0]
        xd_free = 0
        pend_add = None

        add_inc = {}

        def flush_add():
            nonlocal pend_add
            if pend_add is None:
                return
            tt, bbuf, ppe = pend_add
            pend_add = None
            vwait(c, c.s_pe, ppe)
            tmp = c.scr[:, 0:D]
            nc.vector.tensor_tensor(tmp, pf[bbuf][:, :], c.bias_bc[:, :], ALU.add)
            ins = nc.vector.tensor_tensor(c.resid[:, tt * D:(tt + 1) * D],
                                          c.resid[:, tt * D:(tt + 1) * D], tmp,
                                          ALU.add)
            pf_evac[bbuf] = add_inc[tt] = c.s_dve.inc(ins)
            if fuse_next and tt >= 1:
                _ln_accum(c, tt - 1, dve_wait=add_inc[tt - 1])

        for t in range(NT):
            xn_tile = c.xns[:, (t % 2) * D:(t % 2) * D + D]
            if xns_free[t % 2]:
                swait(c, c.s_dve, xns_free[t % 2])
            ac = _ln_apply(c, t, xn_tile)
            vwait(c, c.s_act, ac)
            if xd_free:
                vwait(c, c.s_pe, xd_free)
            dvq = _quant_tile(c, xn_tile, c.qzb[:, :], mid_fn=flush_add)
            xns_free[t % 2] = dvq
            if c.debug == "ffnq" and b == 0 and fi == 0 and t == 0:
                ywait(c, c.s_dve, c.s_dve.n)
                nc.sync.dma_start(c.dbg1[:, 0:1024], c.qzb[:, :]).then_inc(c.d_st.h, 16)
                nc.sync.dma_start(c.dbg1[:, 1024:3072], c.qm[:, :]).then_inc(c.d_st.h, 16)
                nc.sync.dma_start(c.dbg2.bitcast(F32)[:, 0:1024],
                                  xn_tile).then_inc(c.d_st.h, 16)
                nc.sync.dma_start(c.dbg6, c.qs[:, 0:96]).then_inc(c.d_st.h, 16)
                c.d_st.n += 64
                vwait(c, c.d_st, c.d_st.n)
                swait(c, c.d_st, c.d_st.n)
            tp.transpose8(c.qzb[:, :],
                          lambda j: c.xnt[:, j * 512:j * 512 + P],
                          pe_wait=(c.s_dve, dvq))
            xd_free = c.s_pe.n
            dvt = c.s_dve.n
            buf = t % 2
            twait(c, c.s_dve, dvt)
            if t == 0:
                twait(c, c.d_w, wdt_ld)
            if pf_evac[buf]:
                twait(c, c.s_dve, pf_evac[buf])
            mm = None
            for j in range(8):
                lhsT = c.xnt[:, j * 512:j * 512 + P]
                for nn in range(2):
                    mm = nc.tensor.matmul(
                        pf[buf][:, nn * 512:(nn + 1) * 512], lhsT,
                        c.wdq[:, j * D + nn * 512: j * D + (nn + 1) * 512],
                        start=(j == 0), stop=(j == 7))
            pe = c.s_pe.inc(mm)
            pend_add = (t, buf, pe)
        flush_add()
        if fuse_next:
            last = c.s_dve.inc(_gap(c, 2))
            _ln_accum(c, NT - 1, dve_wait=last)


def _attn_b(c, b):
    nc = c.nc
    QB = 256
    _ln_finish(c)
    wvt_ld = _wdq_load(c, c.wvt, c.s_pe.n)

    qt3 = c.qt[:, :].rearrange("p (j s) -> p j s", j=8)
    kt3 = c.kt[:, :].rearrange("p (j s) -> p j s", j=8)

    rope_done = 0
    pv_evac = [0, 0]
    xnb_free = [0, 0]
    for quar in range(NQ):
        with ExitStack() as stk:
            tp = TransPair(c, "tb%d%d" % (b, quar), stk)
            for t_rel in range(TPQ):
                t = quar * TPQ + t_rel
                xn_tile = c.xnb[:, (t % 2) * D:(t % 2) * D + D]
                if xnb_free[t % 2]:
                    swait(c, c.s_pe, xnb_free[t % 2])
                ac = _ln_apply(c, t, xn_tile)
                tp.transpose8(
                    xn_tile,
                    lambda j, t_rel=t_rel: c.xnt[:, j * 512 + t_rel * P:
                                                 j * 512 + t_rel * P + P],
                    pe_wait=(c.s_act, ac))
                xnb_free[t % 2] = c.s_pe.n
            dv_xnt = c.s_dve.inc(_gap(c, 2))
        with nc.psum_tensor("bqA%d%d" % (b, quar), [P, 512], F32) as psA0, \
                nc.psum_tensor("bqB%d%d" % (b, quar), [P, 512], F32) as psB0, \
                nc.psum_tensor("bqC%d%d" % (b, quar), [P, 512], F32) as psA1, \
                nc.psum_tensor("bqD%d%d" % (b, quar), [P, 512], F32) as psB1:
            pspair = ((psA0, psB0), (psA1, psB1))
            rope_of_set = [0, 0]
            cs_used = [0, 0]
            step = 0
            for jp in range(4):
                cslot = jp % 2
                if cs_used[cslot]:
                    gwait(c, c.s_dve, cs_used[cslot])
                gwait(c, c.s_act, c.ln_act_end)
                cssem = c.d_csl[cslot]
                nc.gpsimd.dma_start(
                    c.csb[:, cslot * 1024:cslot * 1024 + 512],
                    c.cst[jp * P:(jp + 1) * P, quar * 512:(quar + 1) * 512]
                ).then_inc(cssem.h, 16)
                cssem.n += 16
                nc.gpsimd.dma_start(
                    c.csb[:, cslot * 1024 + 512:cslot * 1024 + 1024],
                    c.cst[512 + jp * P: 512 + (jp + 1) * P,
                          quar * 512:(quar + 1) * 512]).then_inc(cssem.h, 16)
                cssem.n += 16
                cs_ld = (cssem, cssem.n)
                for wsrc, dst_sb in ((c.wqtp, qt3), (c.wktp, kt3)):
                    sidx = step % 2
                    psA, psB = pspair[sidx]
                    for which, ps in ((jp, psA), (jp + 4, psB)):
                        src = wsrc[:, which * P:(which + 1) * P].rearrange(
                            "(j p) e -> p j e", p=P)
                        slot_ap, wv, slot = _wload(c, src)
                        slot3 = slot_ap.rearrange("p (j e) -> p j e", e=P)
                        twait(c, wv[0], wv[1])
                        twait(c, c.s_dve, dv_xnt)
                        if rope_of_set[sidx]:
                            twait(c, c.s_dve, rope_of_set[sidx])
                        mm = None
                        for j in range(8):
                            mm = nc.tensor.matmul(
                                ps[:, :], slot3[:, j, :],
                                c.xnt[:, j * 512:(j + 1) * 512],
                                start=(j == 0), stop=(j == 7))
                        pe = c.s_pe.inc(mm)
                        c.wfree[slot] = pe
                    pe_pair = c.s_pe.n
                    vwait(c, c.s_pe, pe_pair)
                    vwait(c, cs_ld[0], cs_ld[1])
                    _gap(c, 3)
                    cos = c.csb[:, cslot * 1024:cslot * 1024 + 512]
                    sin = c.csb[:, cslot * 1024 + 512:cslot * 1024 + 1024]
                    t1 = c.scr[:, 0:512]
                    t2 = c.scr[:, 512:1024]
                    dcol = quar * 512
                    nc.vector.tensor_tensor(t1, psA[:, :], cos, ALU.mult)
                    nc.vector.tensor_tensor(t2, psB[:, :], sin, ALU.mult)
                    nc.vector.tensor_tensor(dst_sb[:, jp, dcol:dcol + 512],
                                            t1, t2, ALU.subtract)
                    nc.vector.tensor_tensor(t1, psA[:, :], sin, ALU.mult)
                    nc.vector.tensor_tensor(t2, psB[:, :], cos, ALU.mult)
                    ins = nc.vector.tensor_tensor(dst_sb[:, jp + 4, dcol:dcol + 512],
                                                  t1, t2, ALU.add)
                    rope_of_set[sidx] = rope_done = c.s_dve.inc(ins)
                    cs_used[cslot] = rope_done
                    step += 1
        with nc.psum_tensor("bvA%d%d" % (b, quar), [P, 512], F32) as pvA, \
                nc.psum_tensor("bvB%d%d" % (b, quar), [P, 512], F32) as pvB:
            pv = (pvA, pvB)
            buf = 0
            for t_rel in range(TPQ):
                t = quar * TPQ + t_rel
                lhsT3 = c.xnt[:, :].rearrange("p (j q) -> p j q", q=512)
                for half in range(2):
                    twait(c, c.d_w, wvt_ld)
                    if t_rel == 0 and half == 0:
                        twait(c, c.s_dve, rope_done)
                    if pv_evac[buf]:
                        twait(c, c.s_act, pv_evac[buf])
                    mm = None
                    for j in range(8):
                        mm = nc.tensor.matmul(
                            pv[buf][:, :], lhsT3[:, j, t_rel * P:(t_rel + 1) * P],
                            c.wdq[:, j * D + half * 512: j * D + (half + 1) * 512],
                            start=(j == 0), stop=(j == 7))
                    pe = c.s_pe.inc(mm)
                    swait(c, c.s_pe, pe)
                    pv_evac[buf] = c.s_act.inc(nc.scalar.activation(
                        c.vv[:, t * D + half * 512: t * D + (half + 1) * 512],
                        pv[buf][:, :], AF.Copy))
                    buf = (buf + 1) % 2
    dv_proj = c.s_dve.inc(_gap(c, 3))
    for _ in range(2):
        ac_bv = c.s_act.inc(nc.scalar.activation(c.srows[:, 4:5], c.srows[:, 0:1],
                                                 AF.Square))

    if c.debug == "projb" and b == 0:
        ywait(c, c.s_dve, c.s_dve.n)
        ywait(c, c.s_act, c.s_act.n)
        for dsrc, ddst in ((c.qt[:, :], c.dbg1), (c.kt[:, :], c.dbg2),
                           (c.vv[:, 0:NT * D], c.dbg3[:, 0:NT * D])):
            nc.sync.dma_start(ddst, dsrc).then_inc(c.d_st.h, 16)
            c.d_st.n += 16
        vwait(c, c.d_st, c.d_st.n)
        swait(c, c.d_st, c.d_st.n)
        twait(c, c.d_st, c.d_st.n)

    wot_ld = _wdq_load(c, c.wot, c.s_pe.n)

    with nc.psum_tensor("bsA%d" % b, [P, 512], F32) as sb0, \
            nc.psum_tensor("bsB%d" % b, [P, 512], F32) as sb1, \
            nc.psum_tensor("bo0%d" % b, [P, 512], F32) as ot0, \
            nc.psum_tensor("bo1%d" % b, [P, 512], F32) as ot1, \
            nc.psum_tensor("bo2%d" % b, [P, 512], F32) as ot2, \
            nc.psum_tensor("bo3%d" % b, [P, 512], F32) as ot3, \
            nc.psum_tensor("bdn%d" % b, [1, 512], F32) as den:
        ot = (ot0, ot1, ot2, ot3)
        sb = (sb0, sb1)
        PAIRLAG = 3
        sbuf_last_exp = [0, 0]
        eslot_last_av = [0] * 4
        norm_done = 0
        chain_done = 0
        pend_norm = None
        twait(c, c.s_dve, dv_proj)
        twait(c, c.s_act, ac_bv)
        for qb in range(8):
            exp_of = {}

            def b_scores(kt, qb=qb):
                pair = kt // 2
                buf = sb[pair % 2]
                if kt % 2 == 0 and sbuf_last_exp[pair % 2]:
                    twait(c, c.s_act, sbuf_last_exp[pair % 2])
                half_ = kt % 2
                mm = None
                for j in range(8):
                    lhsT = kt3[:, j, kt * P:(kt + 1) * P]
                    rhs = qt3[:, j, qb * QB:(qb + 1) * QB]
                    mm = nc.tensor.matmul(buf[:, half_ * QB:(half_ + 1) * QB],
                                          lhsT, rhs, start=(j == 0), stop=(j == 7))
                return c.s_pe.inc(mm)

            def b_exp(kt, sc_pe):
                pair = kt // 2
                eslot = pair % 4
                swait(c, c.s_pe, sc_pe)
                if eslot_last_av[eslot]:
                    swait(c, c.s_pe, eslot_last_av[eslot])
                ins = nc.scalar.activation(
                    c.esb[:, eslot * 512:(eslot + 1) * 512],
                    sb[pair % 2][:, :], AF.Exp, bias=_fa(-8.0), scale=_fa(1.0 / 32.0))
                v = c.s_act.inc(ins)
                sbuf_last_exp[pair % 2] = v
                return v

            def b_av(kt, qb=qb):
                pair = kt // 2
                eslot = pair % 4
                twait(c, c.s_act, exp_of.get(pair + 1, exp_tail[0] or exp_of[pair]))
                if kt == 0:
                    twait(c, c.s_dve, norm_done)
                rhs = c.esb[:, eslot * 512 + (kt % 2) * QB:
                            eslot * 512 + (kt % 2) * QB + QB]
                mm = None
                for dpair in range(4):
                    for s_ in range(2):
                        dt_ = dpair * 2 + s_
                        lhsT = c.vv[:, kt * D + dt_ * P: kt * D + dt_ * P + P]
                        mm = nc.tensor.matmul(
                            ot[dpair][:, s_ * QB:(s_ + 1) * QB], lhsT, rhs,
                            start=(kt == 0), stop=(kt == NT - 1))
                mm = nc.tensor.matmul(den[0:1, 0:QB], c.ones[:, 0:1], rhs,
                                      start=(kt == 0), stop=(kt == NT - 1))
                v = c.s_pe.inc(mm)
                eslot_last_av[eslot] = v
                return v

            exp_tail = [0]
            for kt in range(NT):
                sc_pe = b_scores(kt)
                if kt % 2 == 1:
                    exp_of[kt // 2] = b_exp(kt, sc_pe)
                if kt == 4 and pend_norm is not None:
                    norm_done, chain_done = _attn_b_norm(c, *pend_norm, ot, den)
                    pend_norm = None
                if kt >= 2 * PAIRLAG:
                    b_av(kt - 2 * PAIRLAG)
            exp_tail[0] = c.s_act.inc(
                nc.scalar.activation(c.srows[:, 4:5], c.srows[:, 0:1], AF.Square))
            for kt in range(NT - 2 * PAIRLAG, NT):
                b_av(kt)
            pend_norm = (qb, c.s_pe.n)
        norm_done, chain_done = _attn_b_norm(c, *pend_norm, ot, den)

    with nc.psum_tensor("bpA%d" % b, [P, D], F32) as poA, \
            nc.psum_tensor("bpB%d" % b, [P, D], F32) as poB:
        po = (poA, poB)
        po_evac = [0, 0]
        add_inc = {}
        for qt_ in range(NT):
            buf = qt_ % 2
            twait(c, c.d_w, wot_ld)
            if qt_ == 0:
                twait(c, c.s_dve, chain_done)
                twait(c, c.s_act, c.s_act.n)
            if po_evac[buf]:
                twait(c, c.s_dve, po_evac[buf])
            mm = None
            for j in range(8):
                lhsT = qt3[:, j, qt_ * P:(qt_ + 1) * P]
                for nn in range(2):
                    mm = nc.tensor.matmul(
                        po[buf][:, nn * 512:(nn + 1) * 512], lhsT,
                        c.wdq[:, j * D + nn * 512: j * D + (nn + 1) * 512],
                        start=(j == 0), stop=(j == 7))
            pe = c.s_pe.inc(mm)
            vwait(c, c.s_pe, pe)
            ins = nc.vector.tensor_tensor(c.resid[:, qt_ * D:(qt_ + 1) * D],
                                          c.resid[:, qt_ * D:(qt_ + 1) * D],
                                          po[buf][:, :], ALU.add)
            po_evac[buf] = add_inc[qt_] = c.s_dve.inc(ins)
            if qt_ >= 1:
                _ln_accum(c, qt_ - 1, dve_wait=add_inc[qt_ - 1])
        last = c.s_dve.inc(_gap(c, 2))
        _ln_accum(c, NT - 1, dve_wait=last)


def _attn_b_norm(c, qb, pe_after_av, ot, den):
    nc = c.nc
    QB = 256
    osc = c.nsc[:, 0:2048]
    vwait(c, c.s_pe, pe_after_av)
    _gap(c, 8)
    with nc.allow_low_precision(reason="unnorm attn out staged as bf16 operand"):
        for dpair in range(4):
            nc.vector.tensor_copy(osc[:, dpair * 512:(dpair + 1) * 512],
                                  ot[dpair][:, :])
    rr = c.rrow[0:1, 0:QB]
    with nc.allow_low_precision(reason="softmax denom staged bf16 for bcast matmul"):
        nc.vector.tensor_copy(rr, den[0:1, 0:QB])
    norm_done = c.s_dve.inc(_gap(c, 3))
    twait(c, c.s_dve, norm_done)
    c.norm_ctr += 1
    with nc.psum_tensor("bbc%d" % c.norm_ctr, [P, QB], F32) as bcb:
        mm = nc.tensor.matmul(bcb[0:HD, :], c.ones[0:1, 0:HD], rr,
                              start=True, stop=True)
        mm = nc.tensor.matmul(bcb[HD:P, :], c.ones[0:1, 0:HD], rr,
                              start=True, stop=True, tile_position=(0, 64))
        pe = c.s_pe.inc(mm)
        vwait(c, c.s_pe, pe)
        _gap(c, 3)
        with nc.allow_low_precision(reason="softmax denom recip as bf16 operand"):
            nc.vector.reciprocal(c.bcsb[:, 0:QB], bcb[:, :])
        _gap(c, 2)
    qt3 = c.qt[:, :].rearrange("p (j s) -> p j s", j=8)
    for dpair in range(4):
        for s_ in range(2):
            j = dpair * 2 + s_
            nc.vector.tensor_tensor(
                qt3[:, j, qb * QB:(qb + 1) * QB],
                osc[:, dpair * 512 + s_ * QB: dpair * 512 + (s_ + 1) * QB],
                c.bcsb[:, 0:QB], ALU.mult)
    chain_done = c.s_dve.inc(_gap(c, 2))
    return norm_done, chain_done


def _prep_inputs(inputs):
    cst, ident, ones_bf, ones_f32 = _host_constants()
    perm = np.concatenate([np.arange(0, D, 2), np.arange(1, D, 2)])
    com = {
        "wqa": np.ascontiguousarray(np.concatenate(list(inputs["L0_attn_a_W_query"]), axis=1)),
        "wka": np.ascontiguousarray(np.concatenate(list(inputs["L0_attn_a_W_key"]), axis=1)),
        "wva": np.ascontiguousarray(np.concatenate(list(inputs["L0_attn_a_W_val"]), axis=1)),
        "woa": np.ascontiguousarray(np.asarray(inputs["L0_attn_a_W_out"]).reshape(D, D)),
        "wqtp": np.ascontiguousarray(np.asarray(inputs["L0_attn_b_wq"]).T[:, perm]),
        "wktp": np.ascontiguousarray(np.asarray(inputs["L0_attn_b_wk"]).T[:, perm]),
        "wvt": np.ascontiguousarray(np.asarray(inputs["L0_attn_b_wv"]).T),
        "wot": np.ascontiguousarray(np.asarray(inputs["L0_attn_b_wo"]).T),
        "wd": np.ascontiguousarray(np.asarray(inputs["L0_ffn_weight"])),
        "bias": np.ascontiguousarray(np.asarray(inputs["L0_ffn_bias"]).reshape(1, D)),
        "cst": cst,
        "identc": ident,
        "onesb": ones_bf[:, 0:64].copy(),
        "onesf": ones_f32,
    }
    x = np.asarray(inputs["L0_attn_a_q"])
    in_maps = []
    for core in range(NC):
        m = dict(com)
        m["x"] = np.ascontiguousarray(x[NB * core:NB * (core + 1)].reshape(NB * S, D))
        in_maps.append(m)
    return in_maps


def kernel(**inputs):
    in_maps = _prep_inputs(inputs)
    nc = build_program(upto=4)
    res = run_bass_kernel_spmd(nc, in_maps, core_ids=list(range(NC)))
    out = np.empty((NC * NB * S, D), dtype=np.float32)
    for core in range(NC):
        out[core * NB * S:(core + 1) * NB * S] = res.results[core]["out"]
    return out.reshape(16, S, D)
